# version 25
# baseline (speedup 1.0000x reference)
import math
from contextlib import ExitStack

import numpy as np
import ml_dtypes

import concourse.bass as bass
import concourse.mybir as mybir
from concourse.bass_utils import run_bass_kernel_spmd

F32 = mybir.dt.float32
BF16 = mybir.dt.bfloat16
AF = mybir.ActivationFunctionType
ALU = mybir.AluOpType
AX = mybir.AxisListType

f32 = np.float32

B = 2
S = 4096
D = 2048
H = 16
HD = 128
NTILE = 32
NSLOT = 8
TPC = NSLOT * 128
SCALE = f32(1.0 / math.sqrt(HD))
SHIFT = f32(-12.0)

K_TILE_ELEMS = 128 * 16 * 128
V_TILE_ELEMS = 128 * 16 * 129
K_PART = NSLOT * K_TILE_ELEMS
V_PART = NSLOT * V_TILE_ELEMS
A_ELEMS = K_PART + V_PART


class _Ctr:
    def __init__(self, stack, nc, name):
        self.h = stack.enter_context(nc.semaphore(name))
        self.n = 0

    def inc(self, instr, k=1):
        instr.then_inc(self.h, k)
        self.n += k
        return instr


def build_nc():
    nc = bass.Bass("TRN2", target_bir_lowering=False)

    x_h = nc.dram_tensor("x", [TPC, D], F32, kind="ExternalInput").ap()
    wqkv_h = nc.dram_tensor("wqkv", [D, 3 * D], F32, kind="ExternalInput").ap()
    bqkv_h = nc.dram_tensor("bqkv", [3 * D], F32, kind="ExternalInput").ap()
    wout_h = nc.dram_tensor("wout", [D, D], F32, kind="ExternalInput").ap()
    bout_h = nc.dram_tensor("bout", [D], F32, kind="ExternalInput").ap()
    wffn_h = nc.dram_tensor("wffn", [D, D], F32, kind="ExternalInput").ap()
    bffn_h = nc.dram_tensor("bffn", [D], F32, kind="ExternalInput").ap()
    identb_h = nc.dram_tensor("identb", [128, 128], BF16, kind="ExternalInput").ap()
    masks_h = nc.dram_tensor("masks", [NTILE, 128, 128], BF16, kind="ExternalInput").ap()
    out_t = nc.dram_tensor("out", [TPC, D], F32, kind="ExternalOutput")
    out_h = out_t.ap()

    contrib_k = nc.dram_tensor("k_contrib", [K_PART], BF16, kind="Internal")
    contrib_v = nc.dram_tensor("v_contrib", [V_PART], BF16, kind="Internal")
    bsum_d = nc.dram_tensor("bsum_d", [D], F32, kind="Internal")
    rb_d = nc.dram_tensor("rb_d", [2, 512], F32, kind="Internal")
    agout_k = nc.dram_tensor("k_agout", [8, K_PART], BF16, kind="Internal",
                             addr_space="Shared")
    agout_v = nc.dram_tensor("v_agout", [8, V_PART], BF16, kind="Internal",
                             addr_space="Shared")

    with ExitStack() as st:
        e = st.enter_context

        x_res = e(nc.sbuf_tensor("x_res", [128, NSLOT * D], F32))
        q_fm = e(nc.sbuf_tensor("q_fm", [128, H * TPC], BF16))
        bq_sb = e(nc.sbuf_tensor("bq_sb", [128, 48], F32))
        bqv_bf = e(nc.sbuf_tensor("bqv_bf", [128, 16], BF16))
        identb = e(nc.sbuf_tensor("identb_s", [128, 128], BF16))
        ones_f = e(nc.sbuf_tensor("ones_f", [128, 128], F32))

        ld = _Ctr(st, nc, "ld")
        sld = _Ctr(st, nc, "sld")
        xld = [_Ctr(st, nc, f"xld{i}") for i in range(2)]
        s_ln = _Ctr(st, nc, "s_ln")
        v_ln = _Ctr(st, nc, "v_ln")
        pe_tr = _Ctr(st, nc, "pe_tr")
        v_xn = _Ctr(st, nc, "v_xn")
        ld_w = [_Ctr(st, nc, f"ld_w{i}") for i in range(2)]
        cast_p = [_Ctr(st, nc, f"cast_p{i}") for i in range(2)]
        ld_v = [_Ctr(st, nc, f"ld_v{i}") for i in range(2)]
        g_kb = [_Ctr(st, nc, f"g_kb{i}") for i in range(2)]
        cast_v = [_Ctr(st, nc, f"cast_v{i}") for i in range(2)]
        pe_qk = _Ctr(st, nc, "pe_qk")
        s_eqk = _Ctr(st, nc, "s_eqk")
        pe_v = _Ctr(st, nc, "pe_v")
        s_ev = _Ctr(st, nc, "s_ev")
        g_cw = _Ctr(st, nc, "g_cw")
        cc = _Ctr(st, nc, "cc")
        ld_f = [_Ctr(st, nc, f"ld_f{i}") for i in range(3)]
        v_f = _Ctr(st, nc, "v_f")
        s_f = _Ctr(st, nc, "s_f")
        pe_ft = _Ctr(st, nc, "pe_ft")
        v_fev = _Ctr(st, nc, "v_fev")
        pe_fm = _Ctr(st, nc, "pe_fm")
        v_fout = _Ctr(st, nc, "v_fout")
        kld = [_Ctr(st, nc, f"kld{i}") for i in range(4)]
        vld = [_Ctr(st, nc, f"vld{i}") for i in range(8)]
        pe_sc = _Ctr(st, nc, "pe_sc")
        s_exp = _Ctr(st, nc, "s_exp")
        v_mk = _Ctr(st, nc, "v_mk")
        pe_pv = _Ctr(st, nc, "pe_pv")
        v_rr = _Ctr(st, nc, "v_rr")
        g_rr = _Ctr(st, nc, "g_rr")
        pe_bc = _Ctr(st, nc, "pe_bc")
        v_cx = _Ctr(st, nc, "v_cx")
        ld_o = [_Ctr(st, nc, f"ld_o{i}") for i in range(3)]
        v_co = _Ctr(st, nc, "v_co")
        pe_op = _Ctr(st, nc, "pe_op")
        v_op = _Ctr(st, nc, "v_op")
        pe_bv = _Ctr(st, nc, "pe_bv")
        v_bv = _Ctr(st, nc, "v_bv")
        pe_bb = _Ctr(st, nc, "pe_bb")
        v_bb = _Ctr(st, nc, "v_bb")
        g_bs = _Ctr(st, nc, "g_bs")
        g_bs2 = _Ctr(st, nc, "g_bs2")
        v_fin = _Ctr(st, nc, "v_fin")
        stv = _Ctr(st, nc, "stv")

        block = e(nc.Block())

        ph_xn = ExitStack()
        xn_fm = ph_xn.enter_context(nc.sbuf_tensor("xn_fm", [128, 16 * TPC], BF16))
        ph0b = ExitStack()
        bqT_st = ph0b.enter_context(nc.sbuf_tensor("bqT_st", [128, 128], F32))
        identf = ph0b.enter_context(nc.sbuf_tensor("identf", [128, 128], F32))

        def xpart(sync_or_sc, hf):
            xld[hf].inc(sync_or_sc.dma_start(
                out=x_res[:, hf * 4 * D:(hf + 1) * 4 * D].rearrange(
                    "p (n f) -> p n f", f=D),
                in_=x_h[hf * 512:(hf + 1) * 512].rearrange(
                    "(n p) f -> p n f", p=128)), 16)

        @block.sync
        def _(sync):
            ld.inc(sync.dma_start(out=identb[:, :], in_=identb_h[:, :]), 16)
            xpart(sync, 0)

        @block.vector
        def _(v):
            v_ln.inc(v.memset(bqT_st[:, :], 0.0))
            v.wait_ge(ld.h, 16)
            v_ln.inc(v.tensor_copy(identf[:, :], identb[:, :]))

        @block.scalar
        def _(sc):
            xpart(sc, 1)
            sc.wait_ge(v_ln.h, 1)
            sld.inc(sc.dma_start(out=bqT_st[0:48, :],
                                 in_=bqkv_h.rearrange("(f p) -> f p", p=128)), 16)

        ld_total = ld.n

        ph1 = ExitStack()
        xn_tm = ph1.enter_context(nc.sbuf_tensor("xn_tm", [128, NSLOT * D], BF16))
        sq_sc = [ph1.enter_context(nc.sbuf_tensor(f"sq_sc{i}", [128, D], F32)) for i in range(2)]
        stat = ph1.enter_context(nc.sbuf_tensor("stat", [128, 80], F32))
        pt_a = ph1.enter_context(nc.psum_tensor("pt_a", [128, 512], BF16))
        pt_b = ph1.enter_context(nc.psum_tensor("pt_b", [128, 512], BF16))
        bqp = ph1.enter_context(nc.psum_tensor("bqp", [128, 128], F32))

        v_base = v_ln.n
        s_base = s_ln.n
        presqrt_v = [v_base + 11 * hf + 8 for hf in range(2)]
        rec_v = [v_base + 11 * hf + 11 for hf in range(2)]
        sq_done = [s_base + 9 * hf + 4 for hf in range(2)]
        sqrt_done = [s_base + 9 * hf + 5 for hf in range(2)]
        xn_done = [s_base + 9 * (n // 4) + 6 + (n % 4) for n in range(NSLOT)]
        after_sq = [s_base + 9 * (m // 4) + (m % 4) + 1 for m in range(NSLOT)]

        @block.vector
        def _(v):
            for hf in range(2):
                c0 = 4 * hf
                v.wait_ge(xld[hf].h, 16)
                for n in range(c0, c0 + 4):
                    v_ln.inc(v.reduce_sum(stat[:, n:n + 1], x_res[:, n * D:(n + 1) * D],
                                          axis=AX.X))
                v.wait_ge(s_ln.h, sq_done[hf])
                v.wait_ge(v_ln.h, v_base + 11 * hf + 4)
                v_ln.inc(v.tensor_scalar(stat[:, 16 + c0:20 + c0], stat[:, c0:4 + c0],
                                         float(1.0 / D), None, ALU.mult))
                v_ln.inc(v.tensor_scalar(stat[:, 24 + c0:28 + c0], stat[:, 8 + c0:12 + c0],
                                         float(1.0 / D), None, ALU.mult))
                v.wait_ge(v_ln.h, v_ln.n)
                v_ln.inc(v.tensor_mul(stat[:, 32 + c0:36 + c0], stat[:, 16 + c0:20 + c0],
                                      stat[:, 16 + c0:20 + c0]))
                v.wait_ge(v_ln.h, v_ln.n)
                v_ln.inc(v.tensor_sub(stat[:, 24 + c0:28 + c0], stat[:, 24 + c0:28 + c0],
                                      stat[:, 32 + c0:36 + c0]))
                v.wait_ge(s_ln.h, sqrt_done[hf])
                v_ln.inc(v.reciprocal(stat[:, 48 + c0:52 + c0], stat[:, 40 + c0:44 + c0]))
                v_ln.inc(v.tensor_scalar(stat[:, 64 + c0:68 + c0], stat[:, 16 + c0:20 + c0],
                                         -1.0, None, ALU.mult))
                v.wait_ge(v_ln.h, v_ln.n)
                v_ln.inc(v.tensor_mul(stat[:, 56 + c0:60 + c0], stat[:, 64 + c0:68 + c0],
                                      stat[:, 48 + c0:52 + c0]))

        @block.scalar
        def _(sc):
            for hf in range(2):
                c0 = 4 * hf
                for n in range(c0, c0 + 4):
                    sc.wait_ge(xld[hf].h, 16)
                    if n >= 2:
                        sc.wait_ge(s_ln.h, after_sq[n - 2])
                    s_ln.inc(sc.activation(sq_sc[n % 2][:, :], x_res[:, n * D:(n + 1) * D],
                                           AF.Square, accum_out=stat[:, 8 + n:9 + n]))
                sc.wait_ge(v_ln.h, presqrt_v[hf])
                s_ln.inc(sc.activation(stat[:, 40 + c0:44 + c0], stat[:, 24 + c0:28 + c0],
                                       AF.Sqrt, bias=f32(1e-5)))
                sc.wait_ge(v_ln.h, rec_v[hf])
                for n in range(c0, c0 + 4):
                    s_ln.inc(sc.activation(xn_tm[:, n * D:(n + 1) * D],
                                           x_res[:, n * D:(n + 1) * D], AF.Identity,
                                           bias=stat[:, 56 + n:57 + n],
                                           scale=stat[:, 48 + n:49 + n]))

        @block.tensor
        def _(t):
            t.wait_ge(ld.h, ld_total)
            for n in range(NSLOT):
                t.wait_ge(s_ln.h, xn_done[n])
                for g in range(4):
                    gi = n * 4 + g
                    pt = pt_a if gi % 2 == 0 else pt_b
                    if gi >= 2:
                        t.wait_ge(v_xn.h, gi - 1)
                    for i in range(4):
                        phi = g * 4 + i
                        pe_tr.inc(t.transpose(pt[:, i * 128:(i + 1) * 128],
                                              xn_tm[:, n * D + phi * 128:n * D + (phi + 1) * 128],
                                              identb[:, :]))

        @block.vector
        def _(v):
            for n in range(NSLOT):
                for g in range(4):
                    gi = n * 4 + g
                    pt = pt_a if gi % 2 == 0 else pt_b
                    v.wait_ge(pe_tr.h, 4 * (gi + 1))
                    dst = bass.AP(xn_fm, (g * 4) * TPC + n * 128,
                                  [[16 * TPC, 128], [TPC, 4], [1, 128]])
                    v_xn.inc(v.tensor_copy(dst, pt[:, :].rearrange("p (i t) -> p i t", t=128)))

        @block.tensor
        def _(t):
            t.wait_ge(sld.h, 16)
            t.wait_ge(v_ln.h, 2)
            pe_tr.inc(t.transpose(bqp[:, :], bqT_st[:, :], identf[:, :]))
        pe_bq_done = pe_tr.n

        @block.vector
        def _(v):
            v.wait_ge(pe_tr.h, pe_bq_done)
            v_ln.inc(v.tensor_copy(bq_sb[:, :], bqp[:, :48]))
        v_bqc = v_ln.n

        v_xn_total = v_xn.n
        ph1.close()
        ph0b.close()

        @block.vector
        def _(v):
            v_ln.inc(v.memset(ones_f[:, :], 1.0))
            v.wait_ge(v_ln.h, v_bqc)
            v_ln.inc(v.tensor_copy(bqv_bf[:, :], bass.AP(bq_sb, 2, [[48, 128], [3, 16]])))

        ph2 = ExitStack()
        qk_ps = [ph2.enter_context(nc.psum_tensor(f"qk_ps{i}", [128, 512], F32)) for i in range(2)]

        ph2k = ExitStack()
        wstp = [ph2k.enter_context(nc.sbuf_tensor(f"wstp{i}", [128, 16 * 256], F32)) for i in range(2)]
        wsbp = [ph2k.enter_context(nc.sbuf_tensor(f"wsbp{i}", [128, 16 * 256], BF16)) for i in range(2)]
        kbn = [ph2k.enter_context(nc.sbuf_tensor(f"kbn{i}", [128, 512], BF16)) for i in range(2)]

        @block.sync
        def _(sync):
            for i in range(16):
                if i < 2:
                    sync.wait_ge(pe_tr.h, pe_bq_done)
                else:
                    sync.wait_ge(cast_p[(i - 2) % 2].h, (i - 2) // 2 + 1)
                src = wqkv_h.rearrange("(cc p) f -> p cc f", p=128)[
                    :, :, (3 * i) * 128:(3 * i) * 128 + 256]
                ld_w[i % 2].inc(sync.dma_start(
                    out=wstp[i % 2][:, :].rearrange("p (cc f) -> p cc f", f=256),
                    in_=src), 16)

        @block.vector
        def _(v):
            for i in range(16):
                v.wait_ge(ld_w[i % 2].h, 16 * (i // 2 + 1))
                if i >= 2:
                    v.wait_ge(pe_qk.h, 4 * (i - 1))
                cast_p[i % 2].inc(v.tensor_copy(wsbp[i % 2][:, :], wstp[i % 2][:, :]))

        @block.tensor
        def _(t):
            t.wait_ge(v_ln.h, v_bqc)
            for i in range(16):
                t.wait_ge(cast_p[i % 2].h, i // 2 + 1)
                for kq in range(2):
                    off = 128 * (1 - kq)
                    for tb in range(2):
                        if i == 0 and kq == 0:
                            t.wait_ge(v_xn.h, 16 if tb == 0 else v_xn_total)
                        if pe_qk.n >= 2:
                            t.wait_ge(s_eqk.h, pe_qk.n - 1)
                        for cj in range(16):
                            ins = t.matmul(
                                qk_ps[pe_qk.n % 2][:, :],
                                wsbp[i % 2][:, cj * 256 + off:cj * 256 + off + 128],
                                xn_fm[:, cj * TPC + tb * 512:cj * TPC + tb * 512 + 512],
                                start=(cj == 0), stop=(cj == 15))
                        pe_qk.inc(ins)

        @block.scalar
        def _(sc):
            for i in range(16):
                for kq in range(2):
                    F = 3 * i + (1 - kq)
                    for tb in range(2):
                        m = s_eqk.n
                        sc.wait_ge(pe_qk.h, m + 1)
                        if kq == 0:
                            j = 2 * i + tb
                            if j >= 2:
                                sc.wait_ge(g_kb[j % 2].h, 16 * (j // 2))
                            s_eqk.inc(sc.activation(
                                kbn[j % 2][:, :], qk_ps[m % 2][:, :],
                                AF.Identity, bias=bq_sb[:, F:F + 1], scale=f32(1.0)))
                            sc.wait_ge(s_eqk.h, m + 1)
                            dst = bass.AP(contrib_k, (tb * 4) * K_TILE_ELEMS + i * 128,
                                          [[2048, 128], [K_TILE_ELEMS, 4], [1, 128]])
                            g_kb[j % 2].inc(sc.dma_start(
                                out=dst,
                                in_=kbn[j % 2][:, :].rearrange("p (s t) -> p s t", t=128)), 16)
                        else:
                            s_eqk.inc(sc.activation(
                                q_fm[:, i * TPC + tb * 512:i * TPC + tb * 512 + 512],
                                qk_ps[m % 2][:, :], AF.Identity,
                                bias=bq_sb[:, F:F + 1], scale=f32(1.0)))

        n_kq_evac = s_eqk.n
        n_q_evac = n_kq_evac

        @block.gpsimd
        def _(g):
            g.wait_ge(g_kb[0].h, g_kb[0].n)
            g.wait_ge(g_kb[1].h, g_kb[1].n)
            cc.inc(g.collective_compute(
                "AllGather", ALU.bypass, replica_groups=[list(range(8))],
                ins=[contrib_k.ap().opt()], outs=[agout_k.ap().opt()]))

        n_kb = (g_kb[0].n, g_kb[1].n)
        ph2k.close()

        ph2v = ExitStack()
        v_stage = ph2v.enter_context(nc.sbuf_tensor("v_stage", [128, NSLOT * H * 129], BF16))
        wsb_v = ph2v.enter_context(nc.sbuf_tensor("wsb_v", [128, 16 * 512], BF16))
        wst_v = [ph2v.enter_context(nc.sbuf_tensor(f"wst_v{i}", [128, 16 * 128], F32)) for i in range(2)]

        @block.vector
        def _(v):
            v.wait_ge(g_kb[0].h, n_kb[0])
            v.wait_ge(g_kb[1].h, n_kb[1])
            v.wait_ge(pe_qk.h, 64)
            v.wait_ge(cast_p[0].h, 8)
            v.wait_ge(cast_p[1].h, 8)
            v_ln.inc(v.memset(bass.AP(v_stage, 64,
                              [[NSLOT * H * 129, 128], [129, NSLOT * H], [1, 1]]), 1.0))

        v_ones = v_ln.n

        @block.sync
        def _(sync):
            for i in range(16):
                if i == 0:
                    sync.wait_ge(g_kb[0].h, n_kb[0])
                    sync.wait_ge(g_kb[1].h, n_kb[1])
                    sync.wait_ge(pe_qk.h, 64)
                    sync.wait_ge(cast_p[0].h, 8)
                    sync.wait_ge(cast_p[1].h, 8)
                if i >= 2:
                    sync.wait_ge(cast_v[(i - 2) % 2].h, (i - 2) // 2 + 1)
                F = 3 * i + 2
                src = wqkv_h.rearrange("(cc p) f -> p cc f", p=128)[:, :, F * 128:(F + 1) * 128]
                ld_v[i % 2].inc(sync.dma_start(
                    out=wst_v[i % 2][:, :].rearrange("p (cc f) -> p cc f", f=128),
                    in_=src), 16)

        @block.vector
        def _(v):
            for i in range(16):
                v.wait_ge(ld_v[i % 2].h, 16 * (i // 2 + 1))
                G, hh = i // 4, i % 4
                if G >= 1 and hh == 0:
                    v.wait_ge(pe_v.h, 8 * G)
                dst = bass.AP(wsb_v, hh * 128,
                              [[16 * 512, 128], [512, 16], [1, 128]])
                cast_v[i % 2].inc(v.tensor_copy(
                    dst, wst_v[i % 2][:, :].rearrange("p (cc f) -> p cc f", f=128)))

        @block.tensor
        def _(t):
            for G in range(4):
                t.wait_ge(cast_v[0].h, 2 * (G + 1))
                t.wait_ge(cast_v[1].h, 2 * (G + 1))
                for n in range(NSLOT):
                    if pe_v.n >= 2:
                        t.wait_ge(s_ev.h, pe_v.n - 1)
                    else:
                        t.wait_ge(s_eqk.h, n_kq_evac)
                    for cj in range(16):
                        ins = t.matmul(
                            qk_ps[pe_v.n % 2][:, :],
                            xn_fm[:, cj * TPC + n * 128:cj * TPC + (n + 1) * 128],
                            wsb_v[:, cj * 512:(cj + 1) * 512],
                            start=(cj == 0), stop=(cj == 15))
                    pe_v.inc(ins)

        @block.scalar
        def _(sc):
            sc.wait_ge(v_ln.h, v_ones)
            for G in range(4):
                for n in range(NSLOT):
                    sc.wait_ge(pe_v.h, s_ev.n + 1)
                    dst = bass.AP(v_stage, n * (H * 129) + (4 * G) * 129,
                                  [[NSLOT * H * 129, 128], [129, 4], [65, 2], [1, 64]])
                    s_ev.inc(sc.copy(dst, qk_ps[s_ev.n % 2][:, :].rearrange(
                        "p (hh s d) -> p hh s d", s=2, d=64)))

        n_v_evac = s_ev.n

        @block.scalar
        def _(sc):
            sc.wait_ge(s_ev.h, n_v_evac)
            for n in range(NSLOT):
                dst = bass.AP(contrib_v, n * V_TILE_ELEMS, [[2064, 128], [1, 2064]])
                g_cw.inc(sc.dma_start(out=dst, in_=v_stage[:, n * 2064:(n + 1) * 2064]), 16)

        @block.gpsimd
        def _(g):
            g.wait_ge(g_cw.h, g_cw.n)
            cc.inc(g.collective_compute(
                "AllGather", ALU.bypass, replica_groups=[list(range(8))],
                ins=[contrib_v.ap().opt()], outs=[agout_v.ap().opt()]))

        n_cv = g_cw.n
        ph2v.close()

        ph3 = ExitStack()
        ffn_st = [ph3.enter_context(nc.sbuf_tensor(f"ffn_st{i}", [128, D], F32)) for i in range(3)]
        sgn_bf = [ph3.enter_context(nc.sbuf_tensor(f"sgn_bf{i}", [128, D], BF16)) for i in range(3)]
        alph = ph3.enter_context(nc.sbuf_tensor("alph", [128, 16], F32))
        bwT_g = ph3.enter_context(nc.sbuf_tensor("bwT_g", [128, 16 * 512], BF16))
        ptt = [ph3.enter_context(nc.psum_tensor(f"ptt{i}", [128, 512], BF16)) for i in range(2)]
        ffn_ps = [ph3.enter_context(nc.psum_tensor(f"ffn_ps{i}", [128, 512], F32)) for i in range(2)]

        @block.sync
        def _(sync):
            sync.wait_ge(g_cw.h, n_cv)
            for oc in range(16):
                if oc >= 3:
                    sync.wait_ge(s_f.h, oc - 2)
                    sync.wait_ge(v_f.h, 3 * (oc - 2))
                ld_f[oc % 3].inc(sync.dma_start(out=ffn_st[oc % 3][:, :],
                                        in_=wffn_h[oc * 128:(oc + 1) * 128, :]), 16)


        @block.scalar
        def _(sc):
            for oc in range(16):
                sc.wait_ge(ld_f[oc % 3].h, 16 * (oc // 3 + 1))
                if oc >= 3:
                    sc.wait_ge(pe_ft.h, 16 * (oc - 2))
                    sc.wait_ge(v_f.h, 3 * (oc - 2))
                s_f.inc(sc.sign(sgn_bf[oc % 3][:, :], ffn_st[oc % 3][:, :]))

        @block.tensor
        def _(t):
            t.wait_ge(s_eqk.h, n_q_evac)
            t.wait_ge(s_ev.h, n_v_evac)
            ng = 0
            for og in range(4):
                for k in range(4):
                    oc = og * 4 + k
                    t.wait_ge(v_f.h, 3 * (oc + 1))
                    for g in range(4):
                        ng += 1
                        if ng > 2:
                            t.wait_ge(v_fev.h, ng - 2)
                        for i in range(4):
                            ib = g * 4 + i
                            pe_ft.inc(t.transpose(ptt[ng % 2][:, i * 128:(i + 1) * 128],
                                                  sgn_bf[oc % 3][:, ib * 128:(ib + 1) * 128],
                                                  identb[:, :]))
                t.wait_ge(v_fev.h, ng)
                for tt in range(NSLOT):
                    mm_i = og * NSLOT + tt
                    if mm_i >= 2:
                        t.wait_ge(v_fout.h, mm_i - 1)
                    for ic in range(16):
                        ins = t.matmul(
                            ffn_ps[tt % 2][:, :],
                            xn_fm[:, ic * TPC + tt * 128:ic * TPC + (tt + 1) * 128],
                            bwT_g[:, ic * 512:(ic + 1) * 512],
                            start=(ic == 0), stop=(ic == 15))
                    pe_fm.inc(ins)

        @block.vector
        def _(v):
            def _stats(oc):
                v.wait_ge(ld_f[oc % 3].h, 16 * (oc // 3 + 1))
                v_f.inc(v.reduce_sum(alph[:, oc:oc + 1], ffn_st[oc % 3][:, :],
                                     axis=AX.X, apply_absolute_value=True))
                v.wait_ge(v_f.h, v_f.n)
                v_f.inc(v.tensor_scalar(alph[:, oc:oc + 1], alph[:, oc:oc + 1],
                                        float(1.0 / D), None, ALU.mult))
                v.wait_ge(v_f.h, v_f.n)
                v.wait_ge(s_f.h, oc + 1)
                v_f.inc(v.tensor_scalar(sgn_bf[oc % 3][:, :], sgn_bf[oc % 3][:, :],
                                        alph[:, oc:oc + 1], None, ALU.mult))

            for oc in range(3):
                _stats(oc)
            ng = 0
            for og in range(4):
                for k in range(4):
                    oc = og * 4 + k
                    for g in range(4):
                        ng += 1
                        v.wait_ge(pe_ft.h, 4 * ng)
                        if og > 0 and k == 0 and g == 0:
                            v.wait_ge(pe_fm.h, og * NSLOT)
                        dst = bass.AP(bwT_g, (g * 4) * 512 + k * 128,
                                      [[16 * 512, 128], [512, 4], [1, 128]])
                        v_fev.inc(v.tensor_copy(dst, ptt[ng % 2][:, :].rearrange(
                            "p (i o) -> p i o", o=128)))
                    if oc + 3 < 16:
                        _stats(oc + 3)
                for tt in range(NSLOT):
                    mm_i = og * NSLOT + tt
                    v.wait_ge(pe_fm.h, mm_i + 1)
                    v_fout.inc(v.tensor_tensor(
                        x_res[:, tt * D + og * 512:tt * D + og * 512 + 512],
                        x_res[:, tt * D + og * 512:tt * D + og * 512 + 512],
                        ffn_ps[tt % 2][:, :], ALU.add))

        n_fout = 32
        n_fev = 64
        n_sf = 16
        ph3.close()
        ph2.close()
        ph_xn.close()

        ph4s = ExitStack()
        ctx_fm = ph4s.enter_context(nc.sbuf_tensor("ctx_fm", [128, H * TPC], BF16))
        mask_sb = ph4s.enter_context(nc.sbuf_tensor("mask_sb", [128, NTILE * 128], BF16))
        kpair = [ph4s.enter_context(nc.sbuf_tensor(f"kpair{i}", [128, 256], BF16)) for i in range(4)]
        vaug = [ph4s.enter_context(nc.sbuf_tensor(f"vaug{i}", [128, 258], BF16)) for i in range(8)]
        probsT = [ph4s.enter_context(nc.sbuf_tensor(f"probsT{i}", [128, 1024], BF16)) for i in range(3)]
        rr_sb = [ph4s.enter_context(nc.sbuf_tensor(f"rr_sb{i}", [128, 512], F32)) for i in range(2)]
        rr0 = [ph4s.enter_context(nc.sbuf_tensor(f"rr0_{i}", [1, 512], F32)) for i in range(2)]
        rb_sb = [ph4s.enter_context(nc.sbuf_tensor(f"rb_sb{i}", [128, 512], F32)) for i in range(2)]
        wo_st = [ph4s.enter_context(nc.sbuf_tensor(f"wo_st{i}", [128, 512], F32)) for i in range(3)]
        wo_bf = ph4s.enter_context(nc.sbuf_tensor("wo_bf", [128, 16 * 512], BF16))

        ph4p = ExitStack()
        sps = [ph4p.enter_context(nc.psum_tensor(f"sps{i}", [128, 1024], F32)) for i in range(2)]
        ctxp = [[ph4p.enter_context(nc.psum_tensor(f"ctxp{hh}_{i}", [128, 512], F32))
                 for i in range(2)] for hh in range(2)]

        def n_of(c):
            return 512 - (c // 8) * 128

        for pi in range(16):
            hg, b = pi // 2, pi % 2
            h0, h1 = 2 * hg, 2 * hg + 1
            base_sc = pe_sc.n
            base_exp = s_exp.n
            base_mk = v_mk.n
            base_pv = pe_pv.n
            base_cx = v_cx.n
            base_rr_t = 2 * (pi + 1)

            @block.sync
            def _(sync, pi=pi, hg=hg, b=b, base_sc=base_sc):
                if pi == 0:
                    sync.wait_ge(cc.h, 2)
                    sync.wait_ge(s_eqk.h, n_q_evac)
                    sync.wait_ge(s_ev.h, n_v_evac)
                    sync.wait_ge(v_fout.h, n_fout)
                    sync.wait_ge(v_fev.h, n_fev)
                    sync.wait_ge(s_f.h, n_sf)
                    for q in range(4):
                        ld.inc(sync.dma_start(
                            out=mask_sb[:, q * 1024:(q + 1) * 1024].rearrange(
                                "p (c q) -> p c q", q=128),
                            in_=masks_h[q * 8:(q + 1) * 8].transpose([1, 0, 2])), 16)
                def kload(cg):
                    pj, c = cg // NTILE, cg % NTILE
                    hgj, bj = pj // 2, pj % 2
                    tgt = 2 * (cg - 3)
                    if tgt > 0:
                        sync.wait_ge(pe_sc.h, tgt)
                    owner = c % 8
                    tile = bj * 4 + c // 8
                    koff = owner * K_PART + tile * K_TILE_ELEMS + (2 * hgj) * 128
                    kld[c % 4].inc(sync.dma_start(
                        out=kpair[c % 4][:, :],
                        in_=bass.AP(agout_k, koff, [[2048, 128], [1, 256]])), 16)

                if pi == 0:
                    for c in range(4):
                        kload(c)
                for c in range(4, NTILE + 4):
                    cg = pi * NTILE + c
                    if cg < 16 * NTILE:
                        kload(cg)

            def _pv(t, cp, pi=pi, base_mk=base_mk):
                nc_p = n_of(cp)
                cc0 = (cp // 8) * 128
                cg = pi * NTILE + cp
                t.wait_ge(vld[cg % 8].h, 16 * (cg // 8 + 1))
                t.wait_ge(v_mk.h, base_mk + 2 * (cp + 1))
                for hi in range(2):
                    t.matmul(ctxp[hi][0][0:65, cc0:512],
                             vaug[cp % 8][:, hi * 129:hi * 129 + 65],
                             probsT[cp % 3][:, hi * 512:hi * 512 + nc_p],
                             start=(cp == 0), stop=(cp == NTILE - 1))
                    ins = t.matmul(ctxp[hi][1][64:128, cc0:512],
                                   vaug[cp % 8][:, hi * 129 + 65:hi * 129 + 129],
                                   probsT[cp % 3][:, hi * 512:hi * 512 + nc_p],
                                   start=(cp == 0), stop=(cp == NTILE - 1))
                pe_pv.inc(ins)

            @block.tensor
            def _(t, pi=pi, b=b, h0=h0, h1=h1, base_exp=base_exp,
                  base_cx=base_cx, base_rr_t=base_rr_t, _pv=_pv):
                if pi == 0:
                    t.wait_ge(v_cx.h, base_cx)
                    t.wait_ge(v_fout.h, n_fout)
                    t.wait_ge(v_fev.h, n_fev)
                    t.wait_ge(v_ln.h, v_ln.n)
                t.wait_ge(s_exp.h, base_exp)
                for c in range(NTILE):
                    nc_w = n_of(c)
                    cc0 = (c // 8) * 128
                    cgk = pi * NTILE + c
                    t.wait_ge(kld[cgk % 4].h, 16 * (cgk // 4 + 1))
                    if c >= 2:
                        t.wait_ge(s_exp.h, base_exp + c - 1)
                    if pi > 0 and c == 1:
                        t.wait_ge(v_cx.h, base_cx - 2)
                    for hi, hh in enumerate((h0, h1)):
                        qo = hh * TPC + b * 512 + cc0
                        pe_sc.inc(t.matmul(sps[c % 2][:, hi * 512:hi * 512 + nc_w],
                                           kpair[c % 4][:, hi * 128:(hi + 1) * 128],
                                           q_fm[:, qo:qo + nc_w], start=True, stop=True))
                    if c >= 2:
                        if pi > 0 and c == 2:
                            t.wait_ge(v_cx.h, base_cx)
                        _pv(t, c - 2)
                _pv(t, NTILE - 2)
                _pv(t, NTILE - 1)
                for hi in range(2):
                    t.wait_ge(v_rr.h, 2 * pi + hi + 1)
                    pe_bc.inc(t.matmul(sps[1][:, hi * 512:(hi + 1) * 512],
                                       ones_f[64:65, :], rr_sb[hi][64:65, :],
                                       start=True, stop=True))

            @block.scalar
            def _(sc, pi=pi, hg=hg, b=b, base_sc=base_sc, base_pv=base_pv):
                if pi == 0:
                    sc.wait_ge(cc.h, 2)
                    sc.wait_ge(v_fout.h, n_fout)
                    sc.wait_ge(v_fev.h, n_fev)
                def vload(cg):
                    pj, c = cg // NTILE, cg % NTILE
                    hgj, bj = pj // 2, pj % 2
                    tgt = cg - 7
                    if tgt > 0:
                        sc.wait_ge(pe_pv.h, tgt)
                    owner = c % 8
                    tile = bj * 4 + c // 8
                    voff = owner * V_PART + tile * V_TILE_ELEMS + (2 * hgj) * 129
                    vld[c % 8].inc(sc.dma_start(
                        out=vaug[c % 8][:, :],
                        in_=bass.AP(agout_v, voff, [[2064, 128], [1, 258]])), 16)

                if pi == 0:
                    for c in range(4):
                        vload(c)
                for c in range(NTILE):
                    nc_w = n_of(c)
                    cg = pi * NTILE + c + 4
                    if cg < 16 * NTILE:
                        vload(cg)
                    sc.wait_ge(pe_sc.h, base_sc + 2 * (c + 1))
                    if c >= 3:
                        sc.wait_ge(pe_pv.h, base_pv + c - 2)
                    src = bass.AP(sps[c % 2], 0, [[1024, 128], [512, 2], [1, nc_w]])
                    dst = bass.AP(probsT[c % 3], 0, [[1024, 128], [512, 2], [1, nc_w]])
                    s_exp.inc(sc.activation(dst, src, AF.Exp, bias=SHIFT, scale=SCALE))

            @block.vector
            def _(v, pi=pi, hg=hg, b=b, base_exp=base_exp, base_pv=base_pv):
                if pi == 0:
                    v.wait_ge(ld.h, ld.n)

                for c in range(NTILE):
                    v.wait_ge(s_exp.h, base_exp + c + 1)
                    for hi in range(2):
                        v_mk.inc(v.tensor_mul(probsT[c % 3][:, hi * 512:hi * 512 + 128],
                                              probsT[c % 3][:, hi * 512:hi * 512 + 128],
                                              mask_sb[:, c * 128:(c + 1) * 128]))
                v.wait_ge(pe_pv.h, base_pv + NTILE)
                if pi > 0:
                    v.wait_ge(pe_bc.h, 2 * pi)
                for hi in range(2):
                    v_rr.inc(v.reciprocal(rr_sb[hi][64:65, :], ctxp[hi][0][64:65, 0:512]))

            @block.vector
            def _(v, pi=pi, b=b, h0=h0, h1=h1):
                for hi, hh in enumerate((h0, h1)):
                    v.wait_ge(pe_bc.h, 2 * pi + hi + 1)
                    col = hh * TPC + b * 512
                    v_cx.inc(v.tensor_copy(rb_sb[hi][:, :], sps[1][:, hi * 512:(hi + 1) * 512]))
                    v.wait_ge(v_cx.h, v_cx.n)
                    v_cx.inc(v.tensor_mul(ctx_fm[0:64, col:col + 512], ctxp[hi][0][0:64, :],
                                          rb_sb[hi][0:64, :]))
                    v_cx.inc(v.tensor_mul(ctx_fm[64:128, col:col + 512],
                                          ctxp[hi][1][64:128, :], rb_sb[hi][64:128, :]))

        v_cx_total = v_cx.n
        ph4p.close()

        ph5p = ExitStack()
        op_ps = [ph5p.enter_context(nc.psum_tensor(f"op_ps{i}", [128, 512], F32)) for i in range(2)]
        bv_ps = ph5p.enter_context(nc.psum_tensor("bv_ps", [1, 512], F32))
        bc_ps = [ph5p.enter_context(nc.psum_tensor(f"bc_ps{i}", [128, 512], F32)) for i in range(2)]

        @block.sync
        def _(sync):
            for i in range(64):
                if i >= 3:
                    sync.wait_ge(v_co.h, i - 2)
                fg, dc = i // 16, i % 16
                ld_o[i % 3].inc(sync.dma_start(
                    out=wo_st[i % 3][:, :],
                    in_=wout_h[dc * 128:(dc + 1) * 128, fg * 512:(fg + 1) * 512]), 16)


        @block.tensor
        def _(t):
            t.wait_ge(v_cx.h, v_cx_total)
            t.wait_ge(v_ln.h, v_ln.n)
            for fg in range(4):
                for tt in range(NSLOT):
                    mm_i = fg * NSLOT + tt
                    if mm_i >= 2:
                        t.wait_ge(v_op.h, mm_i - 1)
                    for dc in range(16):
                        if tt == 0:
                            t.wait_ge(v_co.h, fg * 16 + dc + 1)
                        ins = t.matmul(
                            op_ps[tt % 2][:, :],
                            ctx_fm[:, dc * TPC + tt * 128:dc * TPC + (tt + 1) * 128],
                            wo_bf[:, dc * 512:(dc + 1) * 512],
                            start=(dc == 0), stop=(dc == 15))
                    pe_op.inc(ins)
                for dc in range(16):
                    if fg > 0 and dc == 0:
                        t.wait_ge(v_bv.h, fg)
                    pe_bv.inc(t.matmul(bv_ps[0:1, :],
                                       bqv_bf[:, dc:dc + 1],
                                       wo_bf[:, dc * 512:(dc + 1) * 512],
                                       start=(dc == 0), stop=(dc == 15)))

        @block.vector
        def _(v):
            for fg in range(4):
                for dc in range(16):
                    i = fg * 16 + dc
                    v.wait_ge(ld_o[i % 3].h, 16 * (i // 3 + 1))
                    if fg >= 1 and dc == 0:
                        v.wait_ge(pe_op.h, fg * NSLOT)
                        v.wait_ge(pe_bv.h, fg * 16)
                    v_co.inc(v.tensor_copy(wo_bf[:, dc * 512:(dc + 1) * 512],
                                           wo_st[i % 3][:, :]))
                for tt in range(NSLOT):
                    mm_i = fg * NSLOT + tt
                    v.wait_ge(pe_op.h, mm_i + 1)
                    v_op.inc(v.tensor_tensor(
                        x_res[:, tt * D + fg * 512:tt * D + fg * 512 + 512],
                        x_res[:, tt * D + fg * 512:tt * D + fg * 512 + 512],
                        op_ps[tt % 2][:, :], ALU.add))
                v.wait_ge(pe_bv.h, (fg + 1) * 16)
                if fg >= 1:
                    v.wait_ge(g_bs2.h, 32 + 16 * fg)
                v_bv.inc(v.tensor_copy(rr0[1][0:1, :], bv_ps[0:1, :]))
            v.wait_ge(v_op.h, v_op.n)
            v_fin.inc(v.engine_nop())

        @block.gpsimd
        def _(g):
            g_bs2.inc(g.dma_start(out=bsum_d.ap(), in_=bout_h), 16)
            g.wait_ge(g_bs2.h, 16)
            g_bs2.inc(g.dma_start(out=bsum_d.ap(), in_=bffn_h, accum_op=ALU.add), 16)
            g.wait_ge(g_bs2.h, 32)
            for fg in range(4):
                g.wait_ge(v_bv.h, fg + 1)
                g_bs2.inc(g.dma_start(out=bsum_d.ap()[fg * 512:(fg + 1) * 512],
                                      in_=rr0[1][0:1, :], accum_op=ALU.add), 16)

        @block.sync
        def _(sync):
            for k in range(4):
                sync.wait_ge(g_bs2.h, 32 + 16 * (k + 1))
                if k >= 1:
                    sync.wait_ge(pe_bb.h, k)
                g_bs.inc(sync.dma_start(out=rr0[0][0:1, :],
                                        in_=bsum_d.ap()[k * 512:(k + 1) * 512]), 16)

        @block.tensor
        def _(t):
            for k in range(4):
                t.wait_ge(g_bs.h, 16 * (k + 1))
                if k >= 2:
                    t.wait_ge(v_bb.h, 8 * (k - 1))
                pe_bb.inc(t.matmul(bc_ps[k % 2][:, :], ones_f[0:1, :],
                                   rr0[0][0:1, :], start=True, stop=True))

        @block.vector
        def _(v):
            for k in range(4):
                v.wait_ge(pe_bb.h, k + 1)
                for n in range(NSLOT):
                    v_bb.inc(v.tensor_tensor(
                        x_res[:, n * D + k * 512:n * D + k * 512 + 512],
                        x_res[:, n * D + k * 512:n * D + k * 512 + 512],
                        bc_ps[k % 2][:, :], ALU.add))

        @block.sync
        def _(sync):
            sync.wait_ge(v_fin.h, v_fin.n)
            for k in range(4):
                sync.wait_ge(v_bb.h, 8 * (k + 1))
                stv.inc(sync.dma_start(
                    out=bass.AP(out_t, k * 512, [[D, 128], [128 * D, 8], [1, 512]]),
                    in_=bass.AP(x_res, k * 512,
                                [[NSLOT * D, 128], [D, 8], [1, 512]])), 16)
            sync.wait_ge(stv.h, stv.n)

        ph5p.close()
        ph4s.close()

    return nc


_NC_CACHE = None


def _get_nc():
    global _NC_CACHE
    if _NC_CACHE is None:
        _NC_CACHE = build_nc()
    return _NC_CACHE


def _make_masks(j):
    m = np.zeros((NTILE, 128, 128), dtype=ml_dtypes.bfloat16)
    tri = (np.arange(128)[:, None] <= np.arange(128)[None, :]).astype(ml_dtypes.bfloat16)
    for c in range(NTILE):
        s0 = c // 8
        g0 = j + 8 * s0
        if c < g0:
            m[c, :, :] = 1.0
        elif c == g0:
            m[c, :, :] = tri
    return m


def kernel(L0_attn_x, L0_attn_W_qkv, L0_attn_b_qkv, L0_attn_W_out, L0_attn_b_out,
           L0_ffn_weight, L0_ffn_bias):
    x = np.ascontiguousarray(np.asarray(L0_attn_x, dtype=np.float32))
    wqkv = np.ascontiguousarray(np.asarray(L0_attn_W_qkv, dtype=np.float32))
    bqkv = np.ascontiguousarray(np.asarray(L0_attn_b_qkv, dtype=np.float32))
    wout = np.ascontiguousarray(np.asarray(L0_attn_W_out, dtype=np.float32))
    bout = np.ascontiguousarray(np.asarray(L0_attn_b_out, dtype=np.float32))
    wffn = np.ascontiguousarray(np.asarray(L0_ffn_weight, dtype=np.float32))
    bffn = np.ascontiguousarray(np.asarray(L0_ffn_bias, dtype=np.float32))

    ident = np.eye(128, dtype=ml_dtypes.bfloat16)

    in_maps = []
    for j in range(8):
        rows = []
        for b in range(B):
            for s in range(4):
                g = j + 8 * s
                rows.append(x[b, g * 128:(g + 1) * 128, :])
        xj = np.ascontiguousarray(np.concatenate(rows, axis=0))
        in_maps.append({
            "x": xj, "wqkv": wqkv, "bqkv": bqkv, "wout": wout, "bout": bout,
            "wffn": wffn, "bffn": bffn, "identb": ident, "masks": _make_masks(j),
        })

    nc = _get_nc()
    res = run_bass_kernel_spmd(nc, in_maps, core_ids=list(range(8)))

    out = np.empty((B, S, D), dtype=np.float32)
    for j in range(8):
        oj = res.results[j]["out"]
        for b in range(B):
            for s in range(4):
                g = j + 8 * s
                slot = b * 4 + s
                out[b, g * 128:(g + 1) * 128, :] = oj[slot * 128:(slot + 1) * 128, :]
    return out


# version 26
# speedup vs baseline: 1.0152x; 1.0152x over previous
import math
from contextlib import ExitStack

import numpy as np
import ml_dtypes

import concourse.bass as bass
import concourse.mybir as mybir
from concourse.bass_utils import run_bass_kernel_spmd

F32 = mybir.dt.float32
BF16 = mybir.dt.bfloat16
AF = mybir.ActivationFunctionType
ALU = mybir.AluOpType
AX = mybir.AxisListType

f32 = np.float32

B = 2
S = 4096
D = 2048
H = 16
HD = 128
NTILE = 32
NSLOT = 8
TPC = NSLOT * 128
SCALE = f32(1.0 / math.sqrt(HD))
SHIFT = f32(-12.0)

K_TILE_ELEMS = 128 * 16 * 128
V_TILE_ELEMS = 128 * 16 * 129
K_PART = NSLOT * K_TILE_ELEMS
V_PART = NSLOT * V_TILE_ELEMS
A_ELEMS = K_PART + V_PART


class _Ctr:
    def __init__(self, stack, nc, name):
        self.h = stack.enter_context(nc.semaphore(name))
        self.n = 0

    def inc(self, instr, k=1):
        instr.then_inc(self.h, k)
        self.n += k
        return instr


def build_nc():
    nc = bass.Bass("TRN2", target_bir_lowering=False)

    x_h = nc.dram_tensor("x", [TPC, D], F32, kind="ExternalInput").ap()
    wqkv_h = nc.dram_tensor("wqkv", [D, 3 * D], F32, kind="ExternalInput").ap()
    bqkv_h = nc.dram_tensor("bqkv", [3 * D], F32, kind="ExternalInput").ap()
    wout_h = nc.dram_tensor("wout", [D, D], F32, kind="ExternalInput").ap()
    bout_h = nc.dram_tensor("bout", [D], F32, kind="ExternalInput").ap()
    wffn_h = nc.dram_tensor("wffn", [D, D], F32, kind="ExternalInput").ap()
    bffn_h = nc.dram_tensor("bffn", [D], F32, kind="ExternalInput").ap()
    identb_h = nc.dram_tensor("identb", [128, 128], BF16, kind="ExternalInput").ap()
    masks_h = nc.dram_tensor("masks", [NTILE, 128, 128], BF16, kind="ExternalInput").ap()
    out_t = nc.dram_tensor("out", [TPC, D], F32, kind="ExternalOutput")
    out_h = out_t.ap()

    contrib_k = nc.dram_tensor("k_contrib", [K_PART], BF16, kind="Internal")
    contrib_v = nc.dram_tensor("v_contrib", [V_PART], BF16, kind="Internal")
    bsum_d = nc.dram_tensor("bsum_d", [D], F32, kind="Internal")
    rb_d = nc.dram_tensor("rb_d", [2, 512], F32, kind="Internal")
    agout_k = nc.dram_tensor("k_agout", [8, K_PART], BF16, kind="Internal",
                             addr_space="Shared")
    agout_v = nc.dram_tensor("v_agout", [8, V_PART], BF16, kind="Internal",
                             addr_space="Shared")

    with ExitStack() as st:
        e = st.enter_context

        x_res = e(nc.sbuf_tensor("x_res", [128, NSLOT * D], F32))
        q_fm = e(nc.sbuf_tensor("q_fm", [128, H * TPC], BF16))
        bq_sb = e(nc.sbuf_tensor("bq_sb", [128, 48], F32))
        bqv_bf = e(nc.sbuf_tensor("bqv_bf", [128, 16], BF16))
        identb = e(nc.sbuf_tensor("identb_s", [128, 128], BF16))
        ones_f = e(nc.sbuf_tensor("ones_f", [128, 128], F32))

        ld = _Ctr(st, nc, "ld")
        sld = _Ctr(st, nc, "sld")
        xld = [_Ctr(st, nc, f"xld{i}") for i in range(4)]
        s_ln = _Ctr(st, nc, "s_ln")
        v_ln = _Ctr(st, nc, "v_ln")
        pe_tr = _Ctr(st, nc, "pe_tr")
        v_xn = _Ctr(st, nc, "v_xn")
        ld_w = [_Ctr(st, nc, f"ld_w{i}") for i in range(2)]
        cast_p = [_Ctr(st, nc, f"cast_p{i}") for i in range(2)]
        ld_v = [_Ctr(st, nc, f"ld_v{i}") for i in range(2)]
        g_kb = [_Ctr(st, nc, f"g_kb{i}") for i in range(2)]
        cast_v = [_Ctr(st, nc, f"cast_v{i}") for i in range(2)]
        pe_qk = _Ctr(st, nc, "pe_qk")
        s_eqk = _Ctr(st, nc, "s_eqk")
        pe_v = _Ctr(st, nc, "pe_v")
        s_ev = _Ctr(st, nc, "s_ev")
        g_cw = _Ctr(st, nc, "g_cw")
        cc = _Ctr(st, nc, "cc")
        ld_f = [_Ctr(st, nc, f"ld_f{i}") for i in range(3)]
        v_f = _Ctr(st, nc, "v_f")
        s_f = _Ctr(st, nc, "s_f")
        pe_ft = _Ctr(st, nc, "pe_ft")
        v_fev = _Ctr(st, nc, "v_fev")
        pe_fm = _Ctr(st, nc, "pe_fm")
        v_fout = _Ctr(st, nc, "v_fout")
        kld = [_Ctr(st, nc, f"kld{i}") for i in range(4)]
        vld = [_Ctr(st, nc, f"vld{i}") for i in range(8)]
        pe_sc = _Ctr(st, nc, "pe_sc")
        s_exp = _Ctr(st, nc, "s_exp")
        v_mk = _Ctr(st, nc, "v_mk")
        pe_pv = _Ctr(st, nc, "pe_pv")
        v_rr = _Ctr(st, nc, "v_rr")
        g_rr = _Ctr(st, nc, "g_rr")
        pe_bc = _Ctr(st, nc, "pe_bc")
        v_cx = _Ctr(st, nc, "v_cx")
        ld_o = [_Ctr(st, nc, f"ld_o{i}") for i in range(3)]
        v_co = _Ctr(st, nc, "v_co")
        pe_op = _Ctr(st, nc, "pe_op")
        v_op = _Ctr(st, nc, "v_op")
        pe_bv = _Ctr(st, nc, "pe_bv")
        v_bv = _Ctr(st, nc, "v_bv")
        pe_bb = _Ctr(st, nc, "pe_bb")
        v_bb = _Ctr(st, nc, "v_bb")
        g_bs = _Ctr(st, nc, "g_bs")
        g_bs2 = _Ctr(st, nc, "g_bs2")
        v_fin = _Ctr(st, nc, "v_fin")
        stv = _Ctr(st, nc, "stv")

        block = e(nc.Block())

        ph_xn = ExitStack()
        xn_fm = ph_xn.enter_context(nc.sbuf_tensor("xn_fm", [128, 16 * TPC], BF16))
        ph2ka = ExitStack()
        wstp0 = ph2ka.enter_context(nc.sbuf_tensor("wstp0", [128, 16 * 256], F32))
        ph2ka2 = ExitStack()
        wstp1 = ph2ka2.enter_context(nc.sbuf_tensor("wstp1", [128, 16 * 256], F32))
        wstp = [wstp0, wstp1]
        ph2kb = ExitStack()
        wsbp = [ph2kb.enter_context(nc.sbuf_tensor(f"wsbp{i}", [128, 16 * 256], BF16)) for i in range(2)]
        kbn = [ph2kb.enter_context(nc.sbuf_tensor(f"kbn{i}", [128, 512], BF16)) for i in range(2)]
        ph0b = ExitStack()
        bqT_st = ph0b.enter_context(nc.sbuf_tensor("bqT_st", [128, 128], F32))
        identf = ph0b.enter_context(nc.sbuf_tensor("identf", [128, 128], F32))

        def xpart(sync_or_sc, q):
            xld[q].inc(sync_or_sc.dma_start(
                out=x_res[:, q * 2 * D:(q + 1) * 2 * D].rearrange(
                    "p (n f) -> p n f", f=D),
                in_=x_h[q * 256:(q + 1) * 256].rearrange(
                    "(n p) f -> p n f", p=128)), 16)

        @block.sync
        def _(sync):
            ld.inc(sync.dma_start(out=identb[:, :], in_=identb_h[:, :]), 16)
            xpart(sync, 0)
            xpart(sync, 2)

        @block.vector
        def _(v):
            v_ln.inc(v.memset(bqT_st[:, :], 0.0))
            v.wait_ge(ld.h, 16)
            v_ln.inc(v.tensor_copy(identf[:, :], identb[:, :]))

        @block.scalar
        def _(sc):
            xpart(sc, 1)
            xpart(sc, 3)
            sc.wait_ge(v_ln.h, 1)
            sld.inc(sc.dma_start(out=bqT_st[0:48, :],
                                 in_=bqkv_h.rearrange("(f p) -> f p", p=128)), 16)

        ld_total = ld.n

        ph1 = ExitStack()
        xn_tm = [ph1.enter_context(nc.sbuf_tensor(f"xn_tm{i}", [128, D], BF16)) for i in range(2)]
        stat = ph1.enter_context(nc.sbuf_tensor("stat", [128, 80], F32))
        pt_a = ph1.enter_context(nc.psum_tensor("pt_a", [128, 512], BF16))
        pt_b = ph1.enter_context(nc.psum_tensor("pt_b", [128, 512], BF16))
        bqp = ph1.enter_context(nc.psum_tensor("bqp", [128, 128], F32))

        v_base = v_ln.n
        s_base = s_ln.n
        presqrt_v = [v_base + 11 * hf + 8 for hf in range(2)]
        rec_v = [v_base + 11 * hf + 11 for hf in range(2)]
        sq_done = [s_base + 4, s_base + 9]
        sqrt_done = [s_base + 5, s_base + 10]
        xn_done = [s_base + 11 + n for n in range(NSLOT)]
        after_sq = [s_base + m + 1 + (1 if m >= 4 else 0) for m in range(NSLOT)]

        @block.vector
        def _(v):
            for hf in range(2):
                c0 = 4 * hf
                for n in range(c0, c0 + 4):
                    v.wait_ge(xld[n // 2].h, 16)
                    v_ln.inc(v.reduce_sum(stat[:, n:n + 1], x_res[:, n * D:(n + 1) * D],
                                          axis=AX.X))
                v.wait_ge(s_ln.h, sq_done[hf])
                v.wait_ge(v_ln.h, v_base + 11 * hf + 4)
                v_ln.inc(v.tensor_scalar(stat[:, 16 + c0:20 + c0], stat[:, c0:4 + c0],
                                         float(1.0 / D), None, ALU.mult))
                v_ln.inc(v.tensor_scalar(stat[:, 24 + c0:28 + c0], stat[:, 8 + c0:12 + c0],
                                         float(1.0 / D), None, ALU.mult))
                v.wait_ge(v_ln.h, v_ln.n)
                v_ln.inc(v.tensor_mul(stat[:, 32 + c0:36 + c0], stat[:, 16 + c0:20 + c0],
                                      stat[:, 16 + c0:20 + c0]))
                v.wait_ge(v_ln.h, v_ln.n)
                v_ln.inc(v.tensor_sub(stat[:, 24 + c0:28 + c0], stat[:, 24 + c0:28 + c0],
                                      stat[:, 32 + c0:36 + c0]))
                v.wait_ge(s_ln.h, sqrt_done[hf])
                v_ln.inc(v.reciprocal(stat[:, 48 + c0:52 + c0], stat[:, 40 + c0:44 + c0]))
                v_ln.inc(v.tensor_scalar(stat[:, 64 + c0:68 + c0], stat[:, 16 + c0:20 + c0],
                                         -1.0, None, ALU.mult))
                v.wait_ge(v_ln.h, v_ln.n)
                v_ln.inc(v.tensor_mul(stat[:, 56 + c0:60 + c0], stat[:, 64 + c0:68 + c0],
                                      stat[:, 48 + c0:52 + c0]))

        @block.scalar
        def _(sc):
            for hf in range(2):
                c0 = 4 * hf
                for n in range(c0, c0 + 4):
                    sc.wait_ge(xld[n // 2].h, 16)
                    if n >= 1:
                        sc.wait_ge(s_ln.h, after_sq[n - 1])
                    s_ln.inc(sc.activation(q_fm[:, 0:D], x_res[:, n * D:(n + 1) * D],
                                           AF.Square, accum_out=stat[:, 8 + n:9 + n]))
                sc.wait_ge(v_ln.h, presqrt_v[hf])
                s_ln.inc(sc.activation(stat[:, 40 + c0:44 + c0], stat[:, 24 + c0:28 + c0],
                                       AF.Sqrt, bias=f32(1e-5)))
            for n in range(NSLOT):
                if n == 0:
                    sc.wait_ge(v_ln.h, rec_v[0])
                elif n == 4:
                    sc.wait_ge(v_ln.h, rec_v[1])
                if n >= 2:
                    sc.wait_ge(pe_tr.h, 16 * (n - 1))
                s_ln.inc(sc.activation(xn_tm[n % 2][:, :],
                                       x_res[:, n * D:(n + 1) * D], AF.Identity,
                                       bias=stat[:, 56 + n:57 + n],
                                       scale=stat[:, 48 + n:49 + n]))

        @block.tensor
        def _(t):
            t.wait_ge(ld.h, ld_total)
            for n in range(NSLOT):
                t.wait_ge(s_ln.h, xn_done[n])
                for g in range(4):
                    gi = n * 4 + g
                    pt = pt_a if gi % 2 == 0 else pt_b
                    if gi >= 2:
                        t.wait_ge(v_xn.h, gi - 1)
                    for i in range(4):
                        phi = g * 4 + i
                        pe_tr.inc(t.transpose(pt[:, i * 128:(i + 1) * 128],
                                              xn_tm[n % 2][:, phi * 128:(phi + 1) * 128],
                                              identb[:, :]))

        @block.vector
        def _(v):
            for n in range(NSLOT):
                for g in range(4):
                    gi = n * 4 + g
                    pt = pt_a if gi % 2 == 0 else pt_b
                    v.wait_ge(pe_tr.h, 4 * (gi + 1))
                    dst = bass.AP(xn_fm, (g * 4) * TPC + n * 128,
                                  [[16 * TPC, 128], [TPC, 4], [1, 128]])
                    v_xn.inc(v.tensor_copy(dst, pt[:, :].rearrange("p (i t) -> p i t", t=128)))

        @block.tensor
        def _(t):
            t.wait_ge(sld.h, 16)
            t.wait_ge(v_ln.h, 2)
            pe_tr.inc(t.transpose(bqp[:, :], bqT_st[:, :], identf[:, :]))
        pe_bq_done = pe_tr.n

        @block.vector
        def _(v):
            v.wait_ge(pe_tr.h, pe_bq_done)
            v_ln.inc(v.tensor_copy(bq_sb[:, :], bqp[:, :48]))
        v_bqc = v_ln.n

        v_xn_total = v_xn.n
        ph1.close()
        ph0b.close()

        @block.vector
        def _(v):
            v_ln.inc(v.memset(ones_f[:, :], 1.0))
            v.wait_ge(v_ln.h, v_bqc)
            v_ln.inc(v.tensor_copy(bqv_bf[:, :], bass.AP(bq_sb, 2, [[48, 128], [3, 16]])))

        ph2 = ExitStack()
        qk_ps = [ph2.enter_context(nc.psum_tensor(f"qk_ps{i}", [128, 512], F32)) for i in range(2)]

        @block.sync
        def _(sync):
            for i in range(16):
                if i >= 2:
                    sync.wait_ge(cast_p[(i - 2) % 2].h, (i - 2) // 2 + 1)
                src = wqkv_h.rearrange("(cc p) f -> p cc f", p=128)[
                    :, :, (3 * i) * 128:(3 * i) * 128 + 256]
                ld_w[i % 2].inc(sync.dma_start(
                    out=wstp[i % 2][:, :].rearrange("p (cc f) -> p cc f", f=256),
                    in_=src), 16)

        @block.vector
        def _(v):
            for i in range(16):
                v.wait_ge(ld_w[i % 2].h, 16 * (i // 2 + 1))
                if i >= 2:
                    v.wait_ge(pe_qk.h, 4 * (i - 1))
                cast_p[i % 2].inc(v.tensor_copy(wsbp[i % 2][:, :], wstp[i % 2][:, :]))

        @block.tensor
        def _(t):
            t.wait_ge(v_ln.h, v_bqc)
            for i in range(16):
                t.wait_ge(cast_p[i % 2].h, i // 2 + 1)
                for kq in range(2):
                    off = 128 * (1 - kq)
                    for tb in range(2):
                        if i == 0 and kq == 0:
                            t.wait_ge(v_xn.h, 16 if tb == 0 else v_xn_total)
                        if pe_qk.n >= 2:
                            t.wait_ge(s_eqk.h, pe_qk.n - 1)
                        for cj in range(16):
                            ins = t.matmul(
                                qk_ps[pe_qk.n % 2][:, :],
                                wsbp[i % 2][:, cj * 256 + off:cj * 256 + off + 128],
                                xn_fm[:, cj * TPC + tb * 512:cj * TPC + tb * 512 + 512],
                                start=(cj == 0), stop=(cj == 15))
                        pe_qk.inc(ins)

        @block.scalar
        def _(sc):
            for i in range(16):
                for kq in range(2):
                    F = 3 * i + (1 - kq)
                    for tb in range(2):
                        m = s_eqk.n
                        sc.wait_ge(pe_qk.h, m + 1)
                        if kq == 0:
                            j = 2 * i + tb
                            if j >= 2:
                                sc.wait_ge(g_kb[j % 2].h, 16 * (j // 2))
                            s_eqk.inc(sc.activation(
                                kbn[j % 2][:, :], qk_ps[m % 2][:, :],
                                AF.Identity, bias=bq_sb[:, F:F + 1], scale=f32(1.0)))
                            sc.wait_ge(s_eqk.h, m + 1)
                            dst = bass.AP(contrib_k, (tb * 4) * K_TILE_ELEMS + i * 128,
                                          [[2048, 128], [K_TILE_ELEMS, 4], [1, 128]])
                            g_kb[j % 2].inc(sc.dma_start(
                                out=dst,
                                in_=kbn[j % 2][:, :].rearrange("p (s t) -> p s t", t=128)), 16)
                        else:
                            s_eqk.inc(sc.activation(
                                q_fm[:, i * TPC + tb * 512:i * TPC + tb * 512 + 512],
                                qk_ps[m % 2][:, :], AF.Identity,
                                bias=bq_sb[:, F:F + 1], scale=f32(1.0)))

        n_kq_evac = s_eqk.n
        n_q_evac = n_kq_evac

        @block.gpsimd
        def _(g):
            g.wait_ge(g_kb[0].h, g_kb[0].n)
            g.wait_ge(g_kb[1].h, g_kb[1].n)
            cc.inc(g.collective_compute(
                "AllGather", ALU.bypass, replica_groups=[list(range(8))],
                ins=[contrib_k.ap().opt()], outs=[agout_k.ap().opt()]))

        n_kb = (g_kb[0].n, g_kb[1].n)
        ph2kb.close()
        ph2ka2.close()

        ph2v = ExitStack()
        v_stage = ph2v.enter_context(nc.sbuf_tensor("v_stage", [128, NSLOT * H * 129], BF16))
        wsb_v = ph2v.enter_context(nc.sbuf_tensor("wsb_v", [128, 16 * 512], BF16))

        @block.vector
        def _(v):
            v.wait_ge(g_kb[0].h, n_kb[0])
            v.wait_ge(g_kb[1].h, n_kb[1])
            v.wait_ge(pe_qk.h, 64)
            v.wait_ge(cast_p[1].h, 8)
            v_ln.inc(v.memset(bass.AP(v_stage, 64,
                              [[NSLOT * H * 129, 128], [129, NSLOT * H], [1, 1]]), 1.0))

        v_ones = v_ln.n

        def vload_w(eng, i):
            if i < 18:
                eng.wait_ge(cast_p[0].h, 8)
            else:
                eng.wait_ge(cast_v[(i - 2) % 2].h, (i - 18) // 2 + 1)
            F = 3 * (i - 16) + 2
            sl = (i % 2) * (16 * 128)
            src = wqkv_h.rearrange("(cc p) f -> p cc f", p=128)[:, :, F * 128:(F + 1) * 128]
            ld_v[i % 2].inc(eng.dma_start(
                out=wstp0[:, sl:sl + 16 * 128].rearrange("p (cc f) -> p cc f", f=128),
                in_=src), 16)

        @block.sync
        def _(sync):
            for i in range(16, 32, 2):
                vload_w(sync, i)

        @block.scalar
        def _(sc):
            for i in (17, 19, 21):
                vload_w(sc, i)

        @block.vector
        def _(v):
            for i in range(16, 32):
                v.wait_ge(ld_v[i % 2].h, 16 * ((i - 16) // 2 + 1))
                G, hh = (i - 16) // 4, (i - 16) % 4
                if i < 18:
                    v.wait_ge(g_kb[0].h, n_kb[0])
                    v.wait_ge(g_kb[1].h, n_kb[1])
                    v.wait_ge(pe_qk.h, 64)
                    v.wait_ge(cast_p[1].h, 8)
                if G >= 1 and hh == 0:
                    v.wait_ge(pe_v.h, 8 * G)
                sl = (i % 2) * (16 * 128)
                dst = bass.AP(wsb_v, hh * 128,
                              [[16 * 512, 128], [512, 16], [1, 128]])
                cast_v[i % 2].inc(v.tensor_copy(
                    dst, wstp0[:, sl:sl + 16 * 128].rearrange("p (cc f) -> p cc f", f=128)))

        @block.tensor
        def _(t):
            for G in range(4):
                t.wait_ge(cast_v[0].h, 2 * (G + 1))
                t.wait_ge(cast_v[1].h, 2 * (G + 1))
                for n in range(NSLOT):
                    if pe_v.n >= 2:
                        t.wait_ge(s_ev.h, pe_v.n - 1)
                    else:
                        t.wait_ge(s_eqk.h, n_kq_evac)
                    for cj in range(16):
                        ins = t.matmul(
                            qk_ps[pe_v.n % 2][:, :],
                            xn_fm[:, cj * TPC + n * 128:cj * TPC + (n + 1) * 128],
                            wsb_v[:, cj * 512:(cj + 1) * 512],
                            start=(cj == 0), stop=(cj == 15))
                    pe_v.inc(ins)

        @block.scalar
        def _(sc):
            sc.wait_ge(v_ln.h, v_ones)
            for G in range(4):
                for n in range(NSLOT):
                    sc.wait_ge(pe_v.h, s_ev.n + 1)
                    dst = bass.AP(v_stage, n * (H * 129) + (4 * G) * 129,
                                  [[NSLOT * H * 129, 128], [129, 4], [65, 2], [1, 64]])
                    s_ev.inc(sc.copy(dst, qk_ps[s_ev.n % 2][:, :].rearrange(
                        "p (hh s d) -> p hh s d", s=2, d=64)))
                for i in ((23, 25), (27, 29), (31,), ())[G]:
                    vload_w(sc, i)

        n_v_evac = s_ev.n

        @block.scalar
        def _(sc):
            sc.wait_ge(s_ev.h, n_v_evac)
            for n in range(NSLOT):
                dst = bass.AP(contrib_v, n * V_TILE_ELEMS, [[2064, 128], [1, 2064]])
                g_cw.inc(sc.dma_start(out=dst, in_=v_stage[:, n * 2064:(n + 1) * 2064]), 16)

        @block.gpsimd
        def _(g):
            g.wait_ge(g_cw.h, g_cw.n)
            cc.inc(g.collective_compute(
                "AllGather", ALU.bypass, replica_groups=[list(range(8))],
                ins=[contrib_v.ap().opt()], outs=[agout_v.ap().opt()]))

        n_cv = g_cw.n
        ph2v.close()
        ph2ka.close()

        ph3 = ExitStack()
        ffn_st = [ph3.enter_context(nc.sbuf_tensor(f"ffn_st{i}", [128, D], F32)) for i in range(3)]
        sgn_bf = [ph3.enter_context(nc.sbuf_tensor(f"sgn_bf{i}", [128, D], BF16)) for i in range(3)]
        alph = ph3.enter_context(nc.sbuf_tensor("alph", [128, 16], F32))
        bwT_g = ph3.enter_context(nc.sbuf_tensor("bwT_g", [128, 16 * 512], BF16))
        ptt = [ph3.enter_context(nc.psum_tensor(f"ptt{i}", [128, 512], BF16)) for i in range(2)]
        ffn_ps = [ph3.enter_context(nc.psum_tensor(f"ffn_ps{i}", [128, 512], F32)) for i in range(2)]

        @block.sync
        def _(sync):
            for oc in range(16):
                if oc < 2:
                    sync.wait_ge(cast_v[0].h, 8)
                    sync.wait_ge(cast_v[1].h, 8)
                elif oc == 2:
                    sync.wait_ge(g_cw.h, n_cv)
                if oc >= 3:
                    sync.wait_ge(s_f.h, oc - 2)
                    sync.wait_ge(v_f.h, 3 * (oc - 2))
                ld_f[oc % 3].inc(sync.dma_start(out=ffn_st[oc % 3][:, :],
                                        in_=wffn_h[oc * 128:(oc + 1) * 128, :]), 16)


        @block.scalar
        def _(sc):
            for oc in range(16):
                sc.wait_ge(ld_f[oc % 3].h, 16 * (oc // 3 + 1))
                if oc == 0:
                    sc.wait_ge(g_cw.h, n_cv)
                if oc >= 3:
                    sc.wait_ge(pe_ft.h, 16 * (oc - 2))
                    sc.wait_ge(v_f.h, 3 * (oc - 2))
                s_f.inc(sc.sign(sgn_bf[oc % 3][:, :], ffn_st[oc % 3][:, :]))

        @block.tensor
        def _(t):
            t.wait_ge(s_eqk.h, n_q_evac)
            t.wait_ge(s_ev.h, n_v_evac)
            ng = 0
            for og in range(4):
                for k in range(4):
                    oc = og * 4 + k
                    t.wait_ge(v_f.h, 3 * (oc + 1))
                    for g in range(4):
                        ng += 1
                        if ng > 2:
                            t.wait_ge(v_fev.h, ng - 2)
                        for i in range(4):
                            ib = g * 4 + i
                            pe_ft.inc(t.transpose(ptt[ng % 2][:, i * 128:(i + 1) * 128],
                                                  sgn_bf[oc % 3][:, ib * 128:(ib + 1) * 128],
                                                  identb[:, :]))
                t.wait_ge(v_fev.h, ng)
                for tt in range(NSLOT):
                    mm_i = og * NSLOT + tt
                    if mm_i >= 2:
                        t.wait_ge(v_fout.h, mm_i - 1)
                    for ic in range(16):
                        ins = t.matmul(
                            ffn_ps[tt % 2][:, :],
                            xn_fm[:, ic * TPC + tt * 128:ic * TPC + (tt + 1) * 128],
                            bwT_g[:, ic * 512:(ic + 1) * 512],
                            start=(ic == 0), stop=(ic == 15))
                    pe_fm.inc(ins)

        @block.vector
        def _(v):
            v.wait_ge(g_cw.h, n_cv)

            def _stats(oc):
                v.wait_ge(ld_f[oc % 3].h, 16 * (oc // 3 + 1))
                v_f.inc(v.reduce_sum(alph[:, oc:oc + 1], ffn_st[oc % 3][:, :],
                                     axis=AX.X, apply_absolute_value=True))
                v.wait_ge(v_f.h, v_f.n)
                v_f.inc(v.tensor_scalar(alph[:, oc:oc + 1], alph[:, oc:oc + 1],
                                        float(1.0 / D), None, ALU.mult))
                v.wait_ge(v_f.h, v_f.n)
                v.wait_ge(s_f.h, oc + 1)
                v_f.inc(v.tensor_scalar(sgn_bf[oc % 3][:, :], sgn_bf[oc % 3][:, :],
                                        alph[:, oc:oc + 1], None, ALU.mult))

            for oc in range(3):
                _stats(oc)
            ng = 0
            for og in range(4):
                for k in range(4):
                    oc = og * 4 + k
                    for g in range(4):
                        ng += 1
                        v.wait_ge(pe_ft.h, 4 * ng)
                        if og > 0 and k == 0 and g == 0:
                            v.wait_ge(pe_fm.h, og * NSLOT)
                        dst = bass.AP(bwT_g, (g * 4) * 512 + k * 128,
                                      [[16 * 512, 128], [512, 4], [1, 128]])
                        v_fev.inc(v.tensor_copy(dst, ptt[ng % 2][:, :].rearrange(
                            "p (i o) -> p i o", o=128)))
                    if oc + 3 < 16:
                        _stats(oc + 3)
                for tt in range(NSLOT):
                    mm_i = og * NSLOT + tt
                    v.wait_ge(pe_fm.h, mm_i + 1)
                    v_fout.inc(v.tensor_tensor(
                        x_res[:, tt * D + og * 512:tt * D + og * 512 + 512],
                        x_res[:, tt * D + og * 512:tt * D + og * 512 + 512],
                        ffn_ps[tt % 2][:, :], ALU.add))

        n_fout = 32
        n_fev = 64
        n_sf = 16
        ph3.close()
        ph2.close()
        ph_xn.close()

        ph4s = ExitStack()
        ctx_fm = ph4s.enter_context(nc.sbuf_tensor("ctx_fm", [128, H * TPC], BF16))
        mask_sb = ph4s.enter_context(nc.sbuf_tensor("mask_sb", [128, NTILE * 128], BF16))
        kpair = [ph4s.enter_context(nc.sbuf_tensor(f"kpair{i}", [128, 256], BF16)) for i in range(4)]
        vaug = [ph4s.enter_context(nc.sbuf_tensor(f"vaug{i}", [128, 258], BF16)) for i in range(8)]
        probsT = [ph4s.enter_context(nc.sbuf_tensor(f"probsT{i}", [128, 1024], BF16)) for i in range(3)]
        rr_sb = [ph4s.enter_context(nc.sbuf_tensor(f"rr_sb{i}", [128, 512], F32)) for i in range(2)]
        rr0 = [ph4s.enter_context(nc.sbuf_tensor(f"rr0_{i}", [1, 512], F32)) for i in range(2)]
        rb_sb = [ph4s.enter_context(nc.sbuf_tensor(f"rb_sb{i}", [128, 512], F32)) for i in range(2)]
        wo_st = [ph4s.enter_context(nc.sbuf_tensor(f"wo_st{i}", [128, 512], F32)) for i in range(3)]
        wo_bf = ph4s.enter_context(nc.sbuf_tensor("wo_bf", [128, 16 * 512], BF16))

        ph4p = ExitStack()
        sps = [ph4p.enter_context(nc.psum_tensor(f"sps{i}", [128, 1024], F32)) for i in range(2)]
        ctxp = [[ph4p.enter_context(nc.psum_tensor(f"ctxp{hh}_{i}", [128, 512], F32))
                 for i in range(2)] for hh in range(2)]

        def n_of(c):
            return 512 - (c // 8) * 128

        for pi in range(16):
            hg, b = pi // 2, pi % 2
            h0, h1 = 2 * hg, 2 * hg + 1
            base_sc = pe_sc.n
            base_exp = s_exp.n
            base_mk = v_mk.n
            base_pv = pe_pv.n
            base_cx = v_cx.n
            base_rr_t = 2 * (pi + 1)

            @block.sync
            def _(sync, pi=pi, hg=hg, b=b, base_sc=base_sc):
                if pi == 0:
                    sync.wait_ge(cc.h, 2)
                    sync.wait_ge(s_eqk.h, n_q_evac)
                    sync.wait_ge(s_ev.h, n_v_evac)
                    sync.wait_ge(v_fout.h, n_fout)
                    sync.wait_ge(v_fev.h, n_fev)
                    sync.wait_ge(s_f.h, n_sf)
                    for q in range(4):
                        ld.inc(sync.dma_start(
                            out=mask_sb[:, q * 1024:(q + 1) * 1024].rearrange(
                                "p (c q) -> p c q", q=128),
                            in_=masks_h[q * 8:(q + 1) * 8].transpose([1, 0, 2])), 16)
                def kload(cg):
                    pj, c = cg // NTILE, cg % NTILE
                    hgj, bj = pj // 2, pj % 2
                    tgt = 2 * (cg - 3)
                    if tgt > 0:
                        sync.wait_ge(pe_sc.h, tgt)
                    owner = c % 8
                    tile = bj * 4 + c // 8
                    koff = owner * K_PART + tile * K_TILE_ELEMS + (2 * hgj) * 128
                    kld[c % 4].inc(sync.dma_start(
                        out=kpair[c % 4][:, :],
                        in_=bass.AP(agout_k, koff, [[2048, 128], [1, 256]])), 16)

                if pi == 0:
                    for c in range(4):
                        kload(c)
                for c in range(4, NTILE + 4):
                    cg = pi * NTILE + c
                    if cg < 16 * NTILE:
                        kload(cg)

            def _pv(t, cp, pi=pi, base_mk=base_mk):
                nc_p = n_of(cp)
                cc0 = (cp // 8) * 128
                cg = pi * NTILE + cp
                t.wait_ge(vld[cg % 8].h, 16 * (cg // 8 + 1))
                t.wait_ge(v_mk.h, base_mk + 2 * (cp + 1))
                for hi in range(2):
                    t.matmul(ctxp[hi][0][0:65, cc0:512],
                             vaug[cp % 8][:, hi * 129:hi * 129 + 65],
                             probsT[cp % 3][:, hi * 512:hi * 512 + nc_p],
                             start=(cp == 0), stop=(cp == NTILE - 1))
                    ins = t.matmul(ctxp[hi][1][64:128, cc0:512],
                                   vaug[cp % 8][:, hi * 129 + 65:hi * 129 + 129],
                                   probsT[cp % 3][:, hi * 512:hi * 512 + nc_p],
                                   start=(cp == 0), stop=(cp == NTILE - 1))
                pe_pv.inc(ins)

            @block.tensor
            def _(t, pi=pi, b=b, h0=h0, h1=h1, base_exp=base_exp,
                  base_cx=base_cx, base_rr_t=base_rr_t, _pv=_pv):
                if pi == 0:
                    t.wait_ge(v_cx.h, base_cx)
                    t.wait_ge(v_fout.h, n_fout)
                    t.wait_ge(v_fev.h, n_fev)
                    t.wait_ge(v_ln.h, v_ln.n)
                t.wait_ge(s_exp.h, base_exp)
                for c in range(NTILE):
                    nc_w = n_of(c)
                    cc0 = (c // 8) * 128
                    cgk = pi * NTILE + c
                    t.wait_ge(kld[cgk % 4].h, 16 * (cgk // 4 + 1))
                    if c >= 2:
                        t.wait_ge(s_exp.h, base_exp + c - 1)
                    if pi > 0 and c == 1:
                        t.wait_ge(v_cx.h, base_cx - 2)
                    for hi, hh in enumerate((h0, h1)):
                        qo = hh * TPC + b * 512 + cc0
                        pe_sc.inc(t.matmul(sps[c % 2][:, hi * 512:hi * 512 + nc_w],
                                           kpair[c % 4][:, hi * 128:(hi + 1) * 128],
                                           q_fm[:, qo:qo + nc_w], start=True, stop=True))
                    if c >= 2:
                        if pi > 0 and c == 2:
                            t.wait_ge(v_cx.h, base_cx)
                        _pv(t, c - 2)
                _pv(t, NTILE - 2)
                _pv(t, NTILE - 1)
                for hi in range(2):
                    t.wait_ge(v_rr.h, 2 * pi + hi + 1)
                    pe_bc.inc(t.matmul(sps[1][:, hi * 512:(hi + 1) * 512],
                                       ones_f[64:65, :], rr_sb[hi][64:65, :],
                                       start=True, stop=True))

            @block.scalar
            def _(sc, pi=pi, hg=hg, b=b, base_sc=base_sc, base_pv=base_pv):
                if pi == 0:
                    sc.wait_ge(cc.h, 2)
                    sc.wait_ge(v_fout.h, n_fout)
                    sc.wait_ge(v_fev.h, n_fev)
                def vload(cg):
                    pj, c = cg // NTILE, cg % NTILE
                    hgj, bj = pj // 2, pj % 2
                    tgt = cg - 7
                    if tgt > 0:
                        sc.wait_ge(pe_pv.h, tgt)
                    owner = c % 8
                    tile = bj * 4 + c // 8
                    voff = owner * V_PART + tile * V_TILE_ELEMS + (2 * hgj) * 129
                    vld[c % 8].inc(sc.dma_start(
                        out=vaug[c % 8][:, :],
                        in_=bass.AP(agout_v, voff, [[2064, 128], [1, 258]])), 16)

                if pi == 0:
                    for c in range(4):
                        vload(c)
                for c in range(NTILE):
                    nc_w = n_of(c)
                    cg = pi * NTILE + c + 4
                    if cg < 16 * NTILE:
                        vload(cg)
                    sc.wait_ge(pe_sc.h, base_sc + 2 * (c + 1))
                    if c >= 3:
                        sc.wait_ge(pe_pv.h, base_pv + c - 2)
                    src = bass.AP(sps[c % 2], 0, [[1024, 128], [512, 2], [1, nc_w]])
                    dst = bass.AP(probsT[c % 3], 0, [[1024, 128], [512, 2], [1, nc_w]])
                    s_exp.inc(sc.activation(dst, src, AF.Exp, bias=SHIFT, scale=SCALE))

            @block.vector
            def _(v, pi=pi, hg=hg, b=b, base_exp=base_exp, base_pv=base_pv):
                if pi == 0:
                    v.wait_ge(ld.h, ld.n)

                for c in range(NTILE):
                    v.wait_ge(s_exp.h, base_exp + c + 1)
                    for hi in range(2):
                        v_mk.inc(v.tensor_mul(probsT[c % 3][:, hi * 512:hi * 512 + 128],
                                              probsT[c % 3][:, hi * 512:hi * 512 + 128],
                                              mask_sb[:, c * 128:(c + 1) * 128]))
                v.wait_ge(pe_pv.h, base_pv + NTILE)
                if pi > 0:
                    v.wait_ge(pe_bc.h, 2 * pi)
                for hi in range(2):
                    v_rr.inc(v.reciprocal(rr_sb[hi][64:65, :], ctxp[hi][0][64:65, 0:512]))

            @block.vector
            def _(v, pi=pi, b=b, h0=h0, h1=h1):
                for hi, hh in enumerate((h0, h1)):
                    v.wait_ge(pe_bc.h, 2 * pi + hi + 1)
                    col = hh * TPC + b * 512
                    v_cx.inc(v.tensor_copy(rb_sb[hi][:, :], sps[1][:, hi * 512:(hi + 1) * 512]))
                    v.wait_ge(v_cx.h, v_cx.n)
                    v_cx.inc(v.tensor_mul(ctx_fm[0:64, col:col + 512], ctxp[hi][0][0:64, :],
                                          rb_sb[hi][0:64, :]))
                    v_cx.inc(v.tensor_mul(ctx_fm[64:128, col:col + 512],
                                          ctxp[hi][1][64:128, :], rb_sb[hi][64:128, :]))

        v_cx_total = v_cx.n
        ph4p.close()

        ph5p = ExitStack()
        op_ps = [ph5p.enter_context(nc.psum_tensor(f"op_ps{i}", [128, 512], F32)) for i in range(2)]
        bv_ps = ph5p.enter_context(nc.psum_tensor("bv_ps", [1, 512], F32))
        bc_ps = [ph5p.enter_context(nc.psum_tensor(f"bc_ps{i}", [128, 512], F32)) for i in range(2)]

        @block.sync
        def _(sync):
            for i in range(64):
                if i >= 3:
                    sync.wait_ge(v_co.h, i - 2)
                fg, dc = i // 16, i % 16
                ld_o[i % 3].inc(sync.dma_start(
                    out=wo_st[i % 3][:, :],
                    in_=wout_h[dc * 128:(dc + 1) * 128, fg * 512:(fg + 1) * 512]), 16)


        @block.tensor
        def _(t):
            t.wait_ge(v_cx.h, v_cx_total)
            t.wait_ge(v_ln.h, v_ln.n)
            for fg in range(4):
                for tt in range(NSLOT):
                    mm_i = fg * NSLOT + tt
                    if mm_i >= 2:
                        t.wait_ge(v_op.h, mm_i - 1)
                    for dc in range(16):
                        if tt == 0:
                            t.wait_ge(v_co.h, fg * 16 + dc + 1)
                        ins = t.matmul(
                            op_ps[tt % 2][:, :],
                            ctx_fm[:, dc * TPC + tt * 128:dc * TPC + (tt + 1) * 128],
                            wo_bf[:, dc * 512:(dc + 1) * 512],
                            start=(dc == 0), stop=(dc == 15))
                    pe_op.inc(ins)
                for dc in range(16):
                    if fg > 0 and dc == 0:
                        t.wait_ge(v_bv.h, fg)
                    pe_bv.inc(t.matmul(bv_ps[0:1, :],
                                       bqv_bf[:, dc:dc + 1],
                                       wo_bf[:, dc * 512:(dc + 1) * 512],
                                       start=(dc == 0), stop=(dc == 15)))

        @block.vector
        def _(v):
            for fg in range(4):
                for dc in range(16):
                    i = fg * 16 + dc
                    v.wait_ge(ld_o[i % 3].h, 16 * (i // 3 + 1))
                    if fg >= 1 and dc == 0:
                        v.wait_ge(pe_op.h, fg * NSLOT)
                        v.wait_ge(pe_bv.h, fg * 16)
                    v_co.inc(v.tensor_copy(wo_bf[:, dc * 512:(dc + 1) * 512],
                                           wo_st[i % 3][:, :]))
                for tt in range(NSLOT):
                    mm_i = fg * NSLOT + tt
                    v.wait_ge(pe_op.h, mm_i + 1)
                    v_op.inc(v.tensor_tensor(
                        x_res[:, tt * D + fg * 512:tt * D + fg * 512 + 512],
                        x_res[:, tt * D + fg * 512:tt * D + fg * 512 + 512],
                        op_ps[tt % 2][:, :], ALU.add))
                v.wait_ge(pe_bv.h, (fg + 1) * 16)
                if fg >= 1:
                    v.wait_ge(g_bs2.h, 32 + 16 * fg)
                v_bv.inc(v.tensor_copy(rr0[1][0:1, :], bv_ps[0:1, :]))
            v.wait_ge(v_op.h, v_op.n)
            v_fin.inc(v.engine_nop())

        @block.gpsimd
        def _(g):
            g_bs2.inc(g.dma_start(out=bsum_d.ap(), in_=bout_h), 16)
            g.wait_ge(g_bs2.h, 16)
            g_bs2.inc(g.dma_start(out=bsum_d.ap(), in_=bffn_h, accum_op=ALU.add), 16)
            g.wait_ge(g_bs2.h, 32)
            for fg in range(4):
                g.wait_ge(v_bv.h, fg + 1)
                g_bs2.inc(g.dma_start(out=bsum_d.ap()[fg * 512:(fg + 1) * 512],
                                      in_=rr0[1][0:1, :], accum_op=ALU.add), 16)

        @block.sync
        def _(sync):
            for k in range(4):
                sync.wait_ge(g_bs2.h, 32 + 16 * (k + 1))
                if k >= 1:
                    sync.wait_ge(pe_bb.h, k)
                g_bs.inc(sync.dma_start(out=rr0[0][0:1, :],
                                        in_=bsum_d.ap()[k * 512:(k + 1) * 512]), 16)

        @block.tensor
        def _(t):
            for k in range(4):
                t.wait_ge(g_bs.h, 16 * (k + 1))
                if k >= 2:
                    t.wait_ge(v_bb.h, 8 * (k - 1))
                pe_bb.inc(t.matmul(bc_ps[k % 2][:, :], ones_f[0:1, :],
                                   rr0[0][0:1, :], start=True, stop=True))

        @block.vector
        def _(v):
            for k in range(4):
                v.wait_ge(pe_bb.h, k + 1)
                for n in range(NSLOT):
                    v_bb.inc(v.tensor_tensor(
                        x_res[:, n * D + k * 512:n * D + k * 512 + 512],
                        x_res[:, n * D + k * 512:n * D + k * 512 + 512],
                        bc_ps[k % 2][:, :], ALU.add))

        @block.sync
        def _(sync):
            sync.wait_ge(v_fin.h, v_fin.n)
            for k in range(4):
                sync.wait_ge(v_bb.h, 8 * (k + 1))
                stv.inc(sync.dma_start(
                    out=bass.AP(out_t, k * 512, [[D, 128], [128 * D, 8], [1, 512]]),
                    in_=bass.AP(x_res, k * 512,
                                [[NSLOT * D, 128], [D, 8], [1, 512]])), 16)
            sync.wait_ge(stv.h, stv.n)

        ph5p.close()
        ph4s.close()

    return nc


_NC_CACHE = None


def _get_nc():
    global _NC_CACHE
    if _NC_CACHE is None:
        _NC_CACHE = build_nc()
    return _NC_CACHE


def _make_masks(j):
    m = np.zeros((NTILE, 128, 128), dtype=ml_dtypes.bfloat16)
    tri = (np.arange(128)[:, None] <= np.arange(128)[None, :]).astype(ml_dtypes.bfloat16)
    for c in range(NTILE):
        s0 = c // 8
        g0 = j + 8 * s0
        if c < g0:
            m[c, :, :] = 1.0
        elif c == g0:
            m[c, :, :] = tri
    return m


def kernel(L0_attn_x, L0_attn_W_qkv, L0_attn_b_qkv, L0_attn_W_out, L0_attn_b_out,
           L0_ffn_weight, L0_ffn_bias):
    x = np.ascontiguousarray(np.asarray(L0_attn_x, dtype=np.float32))
    wqkv = np.ascontiguousarray(np.asarray(L0_attn_W_qkv, dtype=np.float32))
    bqkv = np.ascontiguousarray(np.asarray(L0_attn_b_qkv, dtype=np.float32))
    wout = np.ascontiguousarray(np.asarray(L0_attn_W_out, dtype=np.float32))
    bout = np.ascontiguousarray(np.asarray(L0_attn_b_out, dtype=np.float32))
    wffn = np.ascontiguousarray(np.asarray(L0_ffn_weight, dtype=np.float32))
    bffn = np.ascontiguousarray(np.asarray(L0_ffn_bias, dtype=np.float32))

    ident = np.eye(128, dtype=ml_dtypes.bfloat16)

    in_maps = []
    for j in range(8):
        rows = []
        for b in range(B):
            for s in range(4):
                g = j + 8 * s
                rows.append(x[b, g * 128:(g + 1) * 128, :])
        xj = np.ascontiguousarray(np.concatenate(rows, axis=0))
        in_maps.append({
            "x": xj, "wqkv": wqkv, "bqkv": bqkv, "wout": wout, "bout": bout,
            "wffn": wffn, "bffn": bffn, "identb": ident, "masks": _make_masks(j),
        })

    nc = _get_nc()
    res = run_bass_kernel_spmd(nc, in_maps, core_ids=list(range(8)))

    out = np.empty((B, S, D), dtype=np.float32)
    for j in range(8):
        oj = res.results[j]["out"]
        for b in range(B):
            for s in range(4):
                g = j + 8 * s
                slot = b * 4 + s
                out[b, g * 128:(g + 1) * 128, :] = oj[slot * 128:(slot + 1) * 128, :]
    return out


# version 27 (final)
# speedup vs baseline: 1.0393x; 1.0238x over previous
import math
from contextlib import ExitStack

import numpy as np
import ml_dtypes

import concourse.bass as bass
import concourse.mybir as mybir
from concourse.bass_utils import run_bass_kernel_spmd

F32 = mybir.dt.float32
BF16 = mybir.dt.bfloat16
AF = mybir.ActivationFunctionType
ALU = mybir.AluOpType
AX = mybir.AxisListType

f32 = np.float32

B = 2
S = 4096
D = 2048
H = 16
HD = 128
NTILE = 32
NSLOT = 8
TPC = NSLOT * 128
SCALE = f32(1.0 / math.sqrt(HD))
SHIFT = f32(-12.0)

K_TILE_ELEMS = 128 * 16 * 128
V_TILE_ELEMS = 128 * 16 * 129
K_PART = NSLOT * K_TILE_ELEMS
V_PART = NSLOT * V_TILE_ELEMS
A_ELEMS = K_PART + V_PART


class _Ctr:
    def __init__(self, stack, nc, name):
        self.h = stack.enter_context(nc.semaphore(name))
        self.n = 0

    def inc(self, instr, k=1):
        instr.then_inc(self.h, k)
        self.n += k
        return instr


def build_nc():
    nc = bass.Bass("TRN2", target_bir_lowering=False)

    x_h = nc.dram_tensor("x", [TPC, D], F32, kind="ExternalInput").ap()
    wqkv_h = nc.dram_tensor("wqkv", [D, 3 * D], F32, kind="ExternalInput").ap()
    bqkv_h = nc.dram_tensor("bqkv", [3 * D], F32, kind="ExternalInput").ap()
    wout_h = nc.dram_tensor("wout", [D, D], F32, kind="ExternalInput").ap()
    bout_h = nc.dram_tensor("bout", [D], F32, kind="ExternalInput").ap()
    wffn_h = nc.dram_tensor("wffn", [D, D], F32, kind="ExternalInput").ap()
    bffn_h = nc.dram_tensor("bffn", [D], F32, kind="ExternalInput").ap()
    identb_h = nc.dram_tensor("identb", [128, 128], BF16, kind="ExternalInput").ap()
    masks_h = nc.dram_tensor("masks", [NTILE, 128, 128], BF16, kind="ExternalInput").ap()
    out_t = nc.dram_tensor("out", [TPC, D], F32, kind="ExternalOutput")
    out_h = out_t.ap()

    contrib_k = nc.dram_tensor("k_contrib", [K_PART], BF16, kind="Internal")
    contrib_v = nc.dram_tensor("v_contrib", [V_PART], BF16, kind="Internal")
    bsum_d = nc.dram_tensor("bsum_d", [D], F32, kind="Internal")
    rb_d = nc.dram_tensor("rb_d", [2, 512], F32, kind="Internal")
    agout_k = nc.dram_tensor("k_agout", [8, K_PART], BF16, kind="Internal",
                             addr_space="Shared")
    agout_v = nc.dram_tensor("v_agout", [8, V_PART], BF16, kind="Internal",
                             addr_space="Shared")

    with ExitStack() as st:
        e = st.enter_context

        x_res = e(nc.sbuf_tensor("x_res", [128, NSLOT * D], F32))
        q_fm = e(nc.sbuf_tensor("q_fm", [128, H * TPC], BF16))
        bq_sb = e(nc.sbuf_tensor("bq_sb", [128, 48], F32))
        bqv_bf = e(nc.sbuf_tensor("bqv_bf", [128, 16], BF16))
        identb = e(nc.sbuf_tensor("identb_s", [128, 128], BF16))
        ones_f = e(nc.sbuf_tensor("ones_f", [128, 128], F32))

        ld = _Ctr(st, nc, "ld")
        sld = _Ctr(st, nc, "sld")
        xld = [_Ctr(st, nc, f"xld{i}") for i in range(4)]
        s_ln = _Ctr(st, nc, "s_ln")
        v_ln = _Ctr(st, nc, "v_ln")
        pe_tr = _Ctr(st, nc, "pe_tr")
        v_xn = _Ctr(st, nc, "v_xn")
        ld_w = [_Ctr(st, nc, f"ld_w{i}") for i in range(2)]
        cast_p = [_Ctr(st, nc, f"cast_p{i}") for i in range(2)]
        ld_v = [_Ctr(st, nc, f"ld_v{i}") for i in range(2)]
        g_kb = [_Ctr(st, nc, f"g_kb{i}") for i in range(2)]
        cast_v = [_Ctr(st, nc, f"cast_v{i}") for i in range(2)]
        pe_qk = _Ctr(st, nc, "pe_qk")
        s_eqk = _Ctr(st, nc, "s_eqk")
        pe_v = _Ctr(st, nc, "pe_v")
        s_ev = _Ctr(st, nc, "s_ev")
        g_cw = _Ctr(st, nc, "g_cw")
        cc = _Ctr(st, nc, "cc")
        ld_f = [_Ctr(st, nc, f"ld_f{i}") for i in range(3)]
        v_f = _Ctr(st, nc, "v_f")
        s_f = _Ctr(st, nc, "s_f")
        pe_ft = _Ctr(st, nc, "pe_ft")
        v_fev = _Ctr(st, nc, "v_fev")
        pe_fm = _Ctr(st, nc, "pe_fm")
        v_fout = _Ctr(st, nc, "v_fout")
        kld = [_Ctr(st, nc, f"kld{i}") for i in range(4)]
        vld = [_Ctr(st, nc, f"vld{i}") for i in range(8)]
        pe_sc = _Ctr(st, nc, "pe_sc")
        s_exp = _Ctr(st, nc, "s_exp")
        v_mk = _Ctr(st, nc, "v_mk")
        pe_pv = _Ctr(st, nc, "pe_pv")
        pv_h0 = _Ctr(st, nc, "pv_h0")
        v_rr = _Ctr(st, nc, "v_rr")
        g_rr = _Ctr(st, nc, "g_rr")
        pe_bc = _Ctr(st, nc, "pe_bc")
        v_cx = _Ctr(st, nc, "v_cx")
        ld_o = [_Ctr(st, nc, f"ld_o{i}") for i in range(3)]
        v_co = _Ctr(st, nc, "v_co")
        pe_op = _Ctr(st, nc, "pe_op")
        v_op = _Ctr(st, nc, "v_op")
        pe_bv = _Ctr(st, nc, "pe_bv")
        v_bv = _Ctr(st, nc, "v_bv")
        pe_bb = _Ctr(st, nc, "pe_bb")
        v_bb = _Ctr(st, nc, "v_bb")
        g_bs = _Ctr(st, nc, "g_bs")
        g_bs2 = _Ctr(st, nc, "g_bs2")
        v_fin = _Ctr(st, nc, "v_fin")
        stv = _Ctr(st, nc, "stv")

        block = e(nc.Block())

        ph_xn = ExitStack()
        xn_fm = ph_xn.enter_context(nc.sbuf_tensor("xn_fm", [128, 16 * TPC], BF16))
        ph2ka = ExitStack()
        wstp0 = ph2ka.enter_context(nc.sbuf_tensor("wstp0", [128, 16 * 256], F32))
        ph2ka2 = ExitStack()
        wstp1 = ph2ka2.enter_context(nc.sbuf_tensor("wstp1", [128, 16 * 256], F32))
        wstp = [wstp0, wstp1]
        ph2kb = ExitStack()
        wsbp = [ph2kb.enter_context(nc.sbuf_tensor(f"wsbp{i}", [128, 16 * 256], BF16)) for i in range(2)]
        kbn = [ph2kb.enter_context(nc.sbuf_tensor(f"kbn{i}", [128, 512], BF16)) for i in range(2)]
        ph0b = ExitStack()
        bqT_st = ph0b.enter_context(nc.sbuf_tensor("bqT_st", [128, 128], F32))
        identf = ph0b.enter_context(nc.sbuf_tensor("identf", [128, 128], F32))

        def xpart(sync_or_sc, q):
            xld[q].inc(sync_or_sc.dma_start(
                out=x_res[:, q * 2 * D:(q + 1) * 2 * D].rearrange(
                    "p (n f) -> p n f", f=D),
                in_=x_h[q * 256:(q + 1) * 256].rearrange(
                    "(n p) f -> p n f", p=128)), 16)

        @block.sync
        def _(sync):
            ld.inc(sync.dma_start(out=identb[:, :], in_=identb_h[:, :]), 16)
            xpart(sync, 0)
            xpart(sync, 2)

        @block.vector
        def _(v):
            v_ln.inc(v.memset(bqT_st[:, :], 0.0))
            v.wait_ge(ld.h, 16)
            v_ln.inc(v.tensor_copy(identf[:, :], identb[:, :]))

        @block.scalar
        def _(sc):
            xpart(sc, 1)
            xpart(sc, 3)
            sc.wait_ge(v_ln.h, 1)
            sld.inc(sc.dma_start(out=bqT_st[0:48, :],
                                 in_=bqkv_h.rearrange("(f p) -> f p", p=128)), 16)

        ld_total = ld.n

        ph1 = ExitStack()
        xn_tm = [ph1.enter_context(nc.sbuf_tensor(f"xn_tm{i}", [128, D], BF16)) for i in range(2)]
        stat = ph1.enter_context(nc.sbuf_tensor("stat", [128, 80], F32))
        pt_a = ph1.enter_context(nc.psum_tensor("pt_a", [128, 512], BF16))
        pt_b = ph1.enter_context(nc.psum_tensor("pt_b", [128, 512], BF16))
        bqp = ph1.enter_context(nc.psum_tensor("bqp", [128, 128], F32))

        v_base = v_ln.n
        s_base = s_ln.n
        presqrt_v = [v_base + 11 * hf + 8 for hf in range(2)]
        rec_v = [v_base + 11 * hf + 11 for hf in range(2)]
        sq_done = [s_base + 4, s_base + 9]
        sqrt_done = [s_base + 5, s_base + 10]
        xn_done = [s_base + 11 + n for n in range(NSLOT)]
        after_sq = [s_base + m + 1 + (1 if m >= 4 else 0) for m in range(NSLOT)]

        @block.vector
        def _(v):
            for hf in range(2):
                c0 = 4 * hf
                for n in range(c0, c0 + 4):
                    v.wait_ge(xld[n // 2].h, 16)
                    v_ln.inc(v.reduce_sum(stat[:, n:n + 1], x_res[:, n * D:(n + 1) * D],
                                          axis=AX.X))
                v.wait_ge(s_ln.h, sq_done[hf])
                v.wait_ge(v_ln.h, v_base + 11 * hf + 4)
                v_ln.inc(v.tensor_scalar(stat[:, 16 + c0:20 + c0], stat[:, c0:4 + c0],
                                         float(1.0 / D), None, ALU.mult))
                v_ln.inc(v.tensor_scalar(stat[:, 24 + c0:28 + c0], stat[:, 8 + c0:12 + c0],
                                         float(1.0 / D), None, ALU.mult))
                v.wait_ge(v_ln.h, v_ln.n)
                v_ln.inc(v.tensor_mul(stat[:, 32 + c0:36 + c0], stat[:, 16 + c0:20 + c0],
                                      stat[:, 16 + c0:20 + c0]))
                v.wait_ge(v_ln.h, v_ln.n)
                v_ln.inc(v.tensor_sub(stat[:, 24 + c0:28 + c0], stat[:, 24 + c0:28 + c0],
                                      stat[:, 32 + c0:36 + c0]))
                v.wait_ge(s_ln.h, sqrt_done[hf])
                v_ln.inc(v.reciprocal(stat[:, 48 + c0:52 + c0], stat[:, 40 + c0:44 + c0]))
                v_ln.inc(v.tensor_scalar(stat[:, 64 + c0:68 + c0], stat[:, 16 + c0:20 + c0],
                                         -1.0, None, ALU.mult))
                v.wait_ge(v_ln.h, v_ln.n)
                v_ln.inc(v.tensor_mul(stat[:, 56 + c0:60 + c0], stat[:, 64 + c0:68 + c0],
                                      stat[:, 48 + c0:52 + c0]))

        @block.scalar
        def _(sc):
            for hf in range(2):
                c0 = 4 * hf
                for n in range(c0, c0 + 4):
                    sc.wait_ge(xld[n // 2].h, 16)
                    if n >= 1:
                        sc.wait_ge(s_ln.h, after_sq[n - 1])
                    s_ln.inc(sc.activation(q_fm[:, 0:D], x_res[:, n * D:(n + 1) * D],
                                           AF.Square, accum_out=stat[:, 8 + n:9 + n]))
                sc.wait_ge(v_ln.h, presqrt_v[hf])
                s_ln.inc(sc.activation(stat[:, 40 + c0:44 + c0], stat[:, 24 + c0:28 + c0],
                                       AF.Sqrt, bias=f32(1e-5)))
            for n in range(NSLOT):
                if n == 0:
                    sc.wait_ge(v_ln.h, rec_v[0])
                elif n == 4:
                    sc.wait_ge(v_ln.h, rec_v[1])
                if n >= 2:
                    sc.wait_ge(pe_tr.h, 16 * (n - 1))
                s_ln.inc(sc.activation(xn_tm[n % 2][:, :],
                                       x_res[:, n * D:(n + 1) * D], AF.Identity,
                                       bias=stat[:, 56 + n:57 + n],
                                       scale=stat[:, 48 + n:49 + n]))

        @block.tensor
        def _(t):
            t.wait_ge(ld.h, ld_total)
            for n in range(NSLOT):
                t.wait_ge(s_ln.h, xn_done[n])
                for g in range(4):
                    gi = n * 4 + g
                    pt = pt_a if gi % 2 == 0 else pt_b
                    if gi >= 2:
                        t.wait_ge(v_xn.h, gi - 1)
                    for i in range(4):
                        phi = g * 4 + i
                        pe_tr.inc(t.transpose(pt[:, i * 128:(i + 1) * 128],
                                              xn_tm[n % 2][:, phi * 128:(phi + 1) * 128],
                                              identb[:, :]))

        @block.vector
        def _(v):
            for n in range(NSLOT):
                for g in range(4):
                    gi = n * 4 + g
                    pt = pt_a if gi % 2 == 0 else pt_b
                    v.wait_ge(pe_tr.h, 4 * (gi + 1))
                    dst = bass.AP(xn_fm, (g * 4) * TPC + n * 128,
                                  [[16 * TPC, 128], [TPC, 4], [1, 128]])
                    v_xn.inc(v.tensor_copy(dst, pt[:, :].rearrange("p (i t) -> p i t", t=128)))

        @block.tensor
        def _(t):
            t.wait_ge(sld.h, 16)
            t.wait_ge(v_ln.h, 2)
            pe_tr.inc(t.transpose(bqp[:, :], bqT_st[:, :], identf[:, :]))
        pe_bq_done = pe_tr.n

        @block.vector
        def _(v):
            v.wait_ge(pe_tr.h, pe_bq_done)
            v_ln.inc(v.tensor_copy(bq_sb[:, :], bqp[:, :48]))
        v_bqc = v_ln.n

        v_xn_total = v_xn.n
        ph1.close()
        ph0b.close()

        @block.vector
        def _(v):
            v_ln.inc(v.memset(ones_f[:, :], 1.0))
            v.wait_ge(v_ln.h, v_bqc)
            v_ln.inc(v.tensor_copy(bqv_bf[:, :], bass.AP(bq_sb, 2, [[48, 128], [3, 16]])))

        ph2 = ExitStack()
        qk_ps = [ph2.enter_context(nc.psum_tensor(f"qk_ps{i}", [128, 512], F32)) for i in range(2)]

        @block.sync
        def _(sync):
            for i in range(16):
                if i >= 2:
                    sync.wait_ge(cast_p[(i - 2) % 2].h, (i - 2) // 2 + 1)
                src = wqkv_h.rearrange("(cc p) f -> p cc f", p=128)[
                    :, :, (3 * i) * 128:(3 * i) * 128 + 256]
                ld_w[i % 2].inc(sync.dma_start(
                    out=wstp[i % 2][:, :].rearrange("p (cc f) -> p cc f", f=256),
                    in_=src), 16)

        @block.vector
        def _(v):
            for i in range(16):
                v.wait_ge(ld_w[i % 2].h, 16 * (i // 2 + 1))
                if i >= 2:
                    v.wait_ge(pe_qk.h, 4 * (i - 1))
                cast_p[i % 2].inc(v.tensor_copy(wsbp[i % 2][:, :], wstp[i % 2][:, :]))

        @block.tensor
        def _(t):
            t.wait_ge(v_ln.h, v_bqc)
            for i in range(16):
                t.wait_ge(cast_p[i % 2].h, i // 2 + 1)
                for kq in range(2):
                    off = 128 * (1 - kq)
                    for tb in range(2):
                        if i == 0 and kq == 0:
                            t.wait_ge(v_xn.h, 16 if tb == 0 else v_xn_total)
                        if pe_qk.n >= 2:
                            t.wait_ge(s_eqk.h, pe_qk.n - 1)
                        for cj in range(16):
                            ins = t.matmul(
                                qk_ps[pe_qk.n % 2][:, :],
                                wsbp[i % 2][:, cj * 256 + off:cj * 256 + off + 128],
                                xn_fm[:, cj * TPC + tb * 512:cj * TPC + tb * 512 + 512],
                                start=(cj == 0), stop=(cj == 15))
                        pe_qk.inc(ins)

        @block.scalar
        def _(sc):
            for i in range(16):
                for kq in range(2):
                    F = 3 * i + (1 - kq)
                    for tb in range(2):
                        m = s_eqk.n
                        sc.wait_ge(pe_qk.h, m + 1)
                        if kq == 0:
                            j = 2 * i + tb
                            if j >= 2:
                                sc.wait_ge(g_kb[j % 2].h, 16 * (j // 2))
                            s_eqk.inc(sc.activation(
                                kbn[j % 2][:, :], qk_ps[m % 2][:, :],
                                AF.Identity, bias=bq_sb[:, F:F + 1], scale=f32(1.0)))
                            sc.wait_ge(s_eqk.h, m + 1)
                            dst = bass.AP(contrib_k, (tb * 4) * K_TILE_ELEMS + i * 128,
                                          [[2048, 128], [K_TILE_ELEMS, 4], [1, 128]])
                            g_kb[j % 2].inc(sc.dma_start(
                                out=dst,
                                in_=kbn[j % 2][:, :].rearrange("p (s t) -> p s t", t=128)), 16)
                        else:
                            s_eqk.inc(sc.activation(
                                q_fm[:, i * TPC + tb * 512:i * TPC + tb * 512 + 512],
                                qk_ps[m % 2][:, :], AF.Identity,
                                bias=bq_sb[:, F:F + 1], scale=f32(1.0)))

        n_kq_evac = s_eqk.n
        n_q_evac = n_kq_evac

        @block.gpsimd
        def _(g):
            g.wait_ge(g_kb[0].h, g_kb[0].n)
            g.wait_ge(g_kb[1].h, g_kb[1].n)
            cc.inc(g.collective_compute(
                "AllGather", ALU.bypass, replica_groups=[list(range(8))],
                ins=[contrib_k.ap().opt()], outs=[agout_k.ap().opt()]))

        n_kb = (g_kb[0].n, g_kb[1].n)
        ph2kb.close()
        ph2ka2.close()

        ph2v = ExitStack()
        v_stage = ph2v.enter_context(nc.sbuf_tensor("v_stage", [128, NSLOT * H * 129], BF16))
        wsb_v = ph2v.enter_context(nc.sbuf_tensor("wsb_v", [128, 16 * 512], BF16))

        @block.vector
        def _(v):
            v.wait_ge(g_kb[0].h, n_kb[0])
            v.wait_ge(g_kb[1].h, n_kb[1])
            v.wait_ge(pe_qk.h, 64)
            v.wait_ge(cast_p[1].h, 8)
            v_ln.inc(v.memset(bass.AP(v_stage, 64,
                              [[NSLOT * H * 129, 128], [129, NSLOT * H], [1, 1]]), 1.0))

        v_ones = v_ln.n

        @block.sync
        def _(sync):
            for i in range(16, 32):
                if i < 18:
                    sync.wait_ge(cast_p[0].h, 8)
                else:
                    sync.wait_ge(cast_v[(i - 2) % 2].h, (i - 18) // 2 + 1)
                F = 3 * (i - 16) + 2
                sl = (i % 2) * (16 * 128)
                src = wqkv_h.rearrange("(cc p) f -> p cc f", p=128)[:, :, F * 128:(F + 1) * 128]
                ld_v[i % 2].inc(sync.dma_start(
                    out=wstp0[:, sl:sl + 16 * 128].rearrange("p (cc f) -> p cc f", f=128),
                    in_=src), 16)

        @block.vector
        def _(v):
            for i in range(16, 32):
                v.wait_ge(ld_v[i % 2].h, 16 * ((i - 16) // 2 + 1))
                G, hh = (i - 16) // 4, (i - 16) % 4
                if i < 18:
                    v.wait_ge(g_kb[0].h, n_kb[0])
                    v.wait_ge(g_kb[1].h, n_kb[1])
                    v.wait_ge(pe_qk.h, 64)
                    v.wait_ge(cast_p[1].h, 8)
                if G >= 1 and hh == 0:
                    v.wait_ge(pe_v.h, 8 * G)
                sl = (i % 2) * (16 * 128)
                dst = bass.AP(wsb_v, hh * 128,
                              [[16 * 512, 128], [512, 16], [1, 128]])
                cast_v[i % 2].inc(v.tensor_copy(
                    dst, wstp0[:, sl:sl + 16 * 128].rearrange("p (cc f) -> p cc f", f=128)))

        @block.tensor
        def _(t):
            for G in range(4):
                t.wait_ge(cast_v[0].h, 2 * (G + 1))
                t.wait_ge(cast_v[1].h, 2 * (G + 1))
                for n in range(NSLOT):
                    if pe_v.n >= 2:
                        t.wait_ge(s_ev.h, pe_v.n - 1)
                    else:
                        t.wait_ge(s_eqk.h, n_kq_evac)
                    for cj in range(16):
                        ins = t.matmul(
                            qk_ps[pe_v.n % 2][:, :],
                            xn_fm[:, cj * TPC + n * 128:cj * TPC + (n + 1) * 128],
                            wsb_v[:, cj * 512:(cj + 1) * 512],
                            start=(cj == 0), stop=(cj == 15))
                    pe_v.inc(ins)

        @block.scalar
        def _(sc):
            sc.wait_ge(v_ln.h, v_ones)
            for G in range(4):
                for n in range(NSLOT):
                    sc.wait_ge(pe_v.h, s_ev.n + 1)
                    dst = bass.AP(v_stage, n * (H * 129) + (4 * G) * 129,
                                  [[NSLOT * H * 129, 128], [129, 4], [65, 2], [1, 64]])
                    s_ev.inc(sc.copy(dst, qk_ps[s_ev.n % 2][:, :].rearrange(
                        "p (hh s d) -> p hh s d", s=2, d=64)))

        n_v_evac = s_ev.n

        @block.scalar
        def _(sc):
            sc.wait_ge(s_ev.h, n_v_evac)
            for n in range(NSLOT):
                dst = bass.AP(contrib_v, n * V_TILE_ELEMS, [[2064, 128], [1, 2064]])
                g_cw.inc(sc.dma_start(out=dst, in_=v_stage[:, n * 2064:(n + 1) * 2064]), 16)

        @block.gpsimd
        def _(g):
            g.wait_ge(g_cw.h, g_cw.n)
            cc.inc(g.collective_compute(
                "AllGather", ALU.bypass, replica_groups=[list(range(8))],
                ins=[contrib_v.ap().opt()], outs=[agout_v.ap().opt()]))

        n_cv = g_cw.n
        ph2v.close()
        ph2ka.close()

        ph3 = ExitStack()
        ffn_st = [ph3.enter_context(nc.sbuf_tensor(f"ffn_st{i}", [128, D], F32)) for i in range(3)]
        sgn_bf = [ph3.enter_context(nc.sbuf_tensor(f"sgn_bf{i}", [128, D], BF16)) for i in range(3)]
        alph = ph3.enter_context(nc.sbuf_tensor("alph", [128, 16], F32))
        bwT_g = ph3.enter_context(nc.sbuf_tensor("bwT_g", [128, 16 * 512], BF16))
        ptt = [ph3.enter_context(nc.psum_tensor(f"ptt{i}", [128, 512], BF16)) for i in range(2)]
        ffn_ps = [ph3.enter_context(nc.psum_tensor(f"ffn_ps{i}", [128, 512], F32)) for i in range(2)]

        @block.sync
        def _(sync):
            for oc in range(16):
                if oc < 2:
                    sync.wait_ge(cast_v[0].h, 8)
                    sync.wait_ge(cast_v[1].h, 8)
                elif oc == 2:
                    sync.wait_ge(g_cw.h, n_cv)
                if oc >= 3:
                    sync.wait_ge(s_f.h, oc - 2)
                    sync.wait_ge(v_f.h, 3 * (oc - 2))
                ld_f[oc % 3].inc(sync.dma_start(out=ffn_st[oc % 3][:, :],
                                        in_=wffn_h[oc * 128:(oc + 1) * 128, :]), 16)


        @block.scalar
        def _(sc):
            for oc in range(16):
                sc.wait_ge(ld_f[oc % 3].h, 16 * (oc // 3 + 1))
                if oc == 0:
                    sc.wait_ge(g_cw.h, n_cv)
                if oc >= 3:
                    sc.wait_ge(pe_ft.h, 16 * (oc - 2))
                    sc.wait_ge(v_f.h, 3 * (oc - 2))
                s_f.inc(sc.sign(sgn_bf[oc % 3][:, :], ffn_st[oc % 3][:, :]))

        @block.tensor
        def _(t):
            t.wait_ge(s_eqk.h, n_q_evac)
            t.wait_ge(s_ev.h, n_v_evac)
            ng = 0
            for og in range(4):
                for k in range(4):
                    oc = og * 4 + k
                    t.wait_ge(v_f.h, 3 * (oc + 1))
                    for g in range(4):
                        ng += 1
                        if ng > 2:
                            t.wait_ge(v_fev.h, ng - 2)
                        for i in range(4):
                            ib = g * 4 + i
                            pe_ft.inc(t.transpose(ptt[ng % 2][:, i * 128:(i + 1) * 128],
                                                  sgn_bf[oc % 3][:, ib * 128:(ib + 1) * 128],
                                                  identb[:, :]))
                t.wait_ge(v_fev.h, ng)
                for tt in range(NSLOT):
                    mm_i = og * NSLOT + tt
                    if mm_i >= 2:
                        t.wait_ge(v_fout.h, mm_i - 1)
                    for ic in range(16):
                        ins = t.matmul(
                            ffn_ps[tt % 2][:, :],
                            xn_fm[:, ic * TPC + tt * 128:ic * TPC + (tt + 1) * 128],
                            bwT_g[:, ic * 512:(ic + 1) * 512],
                            start=(ic == 0), stop=(ic == 15))
                    pe_fm.inc(ins)

        @block.vector
        def _(v):
            v.wait_ge(g_cw.h, n_cv)

            def _stats(oc):
                v.wait_ge(ld_f[oc % 3].h, 16 * (oc // 3 + 1))
                v_f.inc(v.reduce_sum(alph[:, oc:oc + 1], ffn_st[oc % 3][:, :],
                                     axis=AX.X, apply_absolute_value=True))
                v.wait_ge(v_f.h, v_f.n)
                v_f.inc(v.tensor_scalar(alph[:, oc:oc + 1], alph[:, oc:oc + 1],
                                        float(1.0 / D), None, ALU.mult))
                v.wait_ge(v_f.h, v_f.n)
                v.wait_ge(s_f.h, oc + 1)
                v_f.inc(v.tensor_scalar(sgn_bf[oc % 3][:, :], sgn_bf[oc % 3][:, :],
                                        alph[:, oc:oc + 1], None, ALU.mult))

            for oc in range(3):
                _stats(oc)
            ng = 0
            for og in range(4):
                for k in range(4):
                    oc = og * 4 + k
                    for g in range(4):
                        ng += 1
                        v.wait_ge(pe_ft.h, 4 * ng)
                        if og > 0 and k == 0 and g == 0:
                            v.wait_ge(pe_fm.h, og * NSLOT)
                        dst = bass.AP(bwT_g, (g * 4) * 512 + k * 128,
                                      [[16 * 512, 128], [512, 4], [1, 128]])
                        v_fev.inc(v.tensor_copy(dst, ptt[ng % 2][:, :].rearrange(
                            "p (i o) -> p i o", o=128)))
                    if oc + 3 < 16:
                        _stats(oc + 3)
                for tt in range(NSLOT):
                    mm_i = og * NSLOT + tt
                    v.wait_ge(pe_fm.h, mm_i + 1)
                    v_fout.inc(v.tensor_tensor(
                        x_res[:, tt * D + og * 512:tt * D + og * 512 + 512],
                        x_res[:, tt * D + og * 512:tt * D + og * 512 + 512],
                        ffn_ps[tt % 2][:, :], ALU.add))

        n_fout = 32
        n_fev = 64
        n_sf = 16
        ph3.close()
        ph2.close()
        ph_xn.close()

        ph4s = ExitStack()
        ctx_fm = ph4s.enter_context(nc.sbuf_tensor("ctx_fm", [128, H * TPC], BF16))
        mask_sb = ph4s.enter_context(nc.sbuf_tensor("mask_sb", [128, NTILE * 128], BF16))
        kpair = [ph4s.enter_context(nc.sbuf_tensor(f"kpair{i}", [128, 256], BF16)) for i in range(4)]
        vaug = [ph4s.enter_context(nc.sbuf_tensor(f"vaug{i}", [128, 258], BF16)) for i in range(8)]
        probsT = [ph4s.enter_context(nc.sbuf_tensor(f"probsT{i}", [128, 1024], BF16)) for i in range(3)]
        rr_sb = [ph4s.enter_context(nc.sbuf_tensor(f"rr_sb{i}", [128, 512], F32)) for i in range(2)]
        rr0 = [ph4s.enter_context(nc.sbuf_tensor(f"rr0_{i}", [1, 512], F32)) for i in range(2)]
        rb_sb = [ph4s.enter_context(nc.sbuf_tensor(f"rb_sb{i}", [128, 512], F32)) for i in range(2)]
        wo_st = [ph4s.enter_context(nc.sbuf_tensor(f"wo_st{i}", [128, 512], F32)) for i in range(3)]
        wo_bf = ph4s.enter_context(nc.sbuf_tensor("wo_bf", [128, 16 * 512], BF16))

        ph4p = ExitStack()
        sps = [ph4p.enter_context(nc.psum_tensor(f"sps{i}", [128, 1024], F32)) for i in range(2)]
        ctxp = [[ph4p.enter_context(nc.psum_tensor(f"ctxp{hh}_{i}", [128, 512], F32))
                 for i in range(2)] for hh in range(2)]

        def n_of(c):
            return 512 - (c // 8) * 128

        for pi in range(16):
            hg, b = pi // 2, pi % 2
            h0, h1 = 2 * hg, 2 * hg + 1
            base_sc = pe_sc.n
            base_exp = s_exp.n
            base_mk = v_mk.n
            base_pv = pe_pv.n
            base_cx = v_cx.n
            base_rr_t = 2 * (pi + 1)

            @block.sync
            def _(sync, pi=pi, hg=hg, b=b, base_sc=base_sc):
                if pi == 0:
                    sync.wait_ge(cc.h, 2)
                    sync.wait_ge(s_eqk.h, n_q_evac)
                    sync.wait_ge(s_ev.h, n_v_evac)
                    sync.wait_ge(v_fout.h, n_fout)
                    sync.wait_ge(v_fev.h, n_fev)
                    sync.wait_ge(s_f.h, n_sf)
                    for q in range(4):
                        ld.inc(sync.dma_start(
                            out=mask_sb[:, q * 1024:(q + 1) * 1024].rearrange(
                                "p (c q) -> p c q", q=128),
                            in_=masks_h[q * 8:(q + 1) * 8].transpose([1, 0, 2])), 16)
                def kload(cg):
                    pj, c = cg // NTILE, cg % NTILE
                    hgj, bj = pj // 2, pj % 2
                    tgt = 2 * (cg - 3)
                    if tgt > 0:
                        sync.wait_ge(pe_sc.h, tgt)
                    owner = c % 8
                    tile = bj * 4 + c // 8
                    koff = owner * K_PART + tile * K_TILE_ELEMS + (2 * hgj) * 128
                    kld[c % 4].inc(sync.dma_start(
                        out=kpair[c % 4][:, :],
                        in_=bass.AP(agout_k, koff, [[2048, 128], [1, 256]])), 16)

                if pi == 0:
                    for c in range(4):
                        kload(c)
                for c in range(4, NTILE + 4):
                    cg = pi * NTILE + c
                    if cg < 16 * NTILE:
                        kload(cg)

            def _pv(t, cp, pi=pi, base_mk=base_mk):
                nc_p = n_of(cp)
                cc0 = (cp // 8) * 128
                cg = pi * NTILE + cp
                t.wait_ge(vld[cg % 8].h, 16 * (cg // 8 + 1))
                t.wait_ge(v_mk.h, base_mk + 2 * (cp + 1))
                for hi in range(2):
                    t.matmul(ctxp[hi][0][0:65, cc0:512],
                             vaug[cp % 8][:, hi * 129:hi * 129 + 65],
                             probsT[cp % 3][:, hi * 512:hi * 512 + nc_p],
                             start=(cp == 0), stop=(cp == NTILE - 1))
                    ins = t.matmul(ctxp[hi][1][64:128, cc0:512],
                                   vaug[cp % 8][:, hi * 129 + 65:hi * 129 + 129],
                                   probsT[cp % 3][:, hi * 512:hi * 512 + nc_p],
                                   start=(cp == 0), stop=(cp == NTILE - 1))
                    if cp == NTILE - 1 and hi == 0:
                        pv_h0.inc(ins)
                pe_pv.inc(ins)

            @block.tensor
            def _(t, pi=pi, b=b, h0=h0, h1=h1, base_exp=base_exp,
                  base_cx=base_cx, base_rr_t=base_rr_t, _pv=_pv):
                if pi == 0:
                    t.wait_ge(v_cx.h, base_cx)
                    t.wait_ge(v_fout.h, n_fout)
                    t.wait_ge(v_fev.h, n_fev)
                    t.wait_ge(v_ln.h, v_ln.n)
                t.wait_ge(s_exp.h, base_exp)
                for c in range(NTILE):
                    nc_w = n_of(c)
                    cc0 = (c // 8) * 128
                    cgk = pi * NTILE + c
                    t.wait_ge(kld[cgk % 4].h, 16 * (cgk // 4 + 1))
                    if c >= 2:
                        t.wait_ge(s_exp.h, base_exp + c - 1)
                    if pi > 0 and c == 1:
                        t.wait_ge(v_cx.h, base_cx - 2)
                    for hi, hh in enumerate((h0, h1)):
                        qo = hh * TPC + b * 512 + cc0
                        pe_sc.inc(t.matmul(sps[c % 2][:, hi * 512:hi * 512 + nc_w],
                                           kpair[c % 4][:, hi * 128:(hi + 1) * 128],
                                           q_fm[:, qo:qo + nc_w], start=True, stop=True))
                    if c >= 2:
                        if pi > 0 and c == 2:
                            t.wait_ge(v_cx.h, base_cx)
                        _pv(t, c - 2)
                _pv(t, NTILE - 2)
                _pv(t, NTILE - 1)
                for hi in range(2):
                    t.wait_ge(v_rr.h, 2 * pi + hi + 1)
                    pe_bc.inc(t.matmul(sps[1][:, hi * 512:(hi + 1) * 512],
                                       ones_f[64:65, :], rr_sb[hi][64:65, :],
                                       start=True, stop=True))

            @block.scalar
            def _(sc, pi=pi, hg=hg, b=b, base_sc=base_sc, base_pv=base_pv):
                if pi == 0:
                    sc.wait_ge(cc.h, 2)
                    sc.wait_ge(v_fout.h, n_fout)
                    sc.wait_ge(v_fev.h, n_fev)
                def vload(cg):
                    pj, c = cg // NTILE, cg % NTILE
                    hgj, bj = pj // 2, pj % 2
                    tgt = cg - 7
                    if tgt > 0:
                        sc.wait_ge(pe_pv.h, tgt)
                    owner = c % 8
                    tile = bj * 4 + c // 8
                    voff = owner * V_PART + tile * V_TILE_ELEMS + (2 * hgj) * 129
                    vld[c % 8].inc(sc.dma_start(
                        out=vaug[c % 8][:, :],
                        in_=bass.AP(agout_v, voff, [[2064, 128], [1, 258]])), 16)

                if pi == 0:
                    for c in range(4):
                        vload(c)
                for c in range(NTILE):
                    nc_w = n_of(c)
                    cg = pi * NTILE + c + 4
                    if cg < 16 * NTILE:
                        vload(cg)
                    sc.wait_ge(pe_sc.h, base_sc + 2 * (c + 1))
                    if c >= 3:
                        sc.wait_ge(pe_pv.h, base_pv + c - 2)
                    src = bass.AP(sps[c % 2], 0, [[1024, 128], [512, 2], [1, nc_w]])
                    dst = bass.AP(probsT[c % 3], 0, [[1024, 128], [512, 2], [1, nc_w]])
                    s_exp.inc(sc.activation(dst, src, AF.Exp, bias=SHIFT, scale=SCALE))

            @block.vector
            def _(v, pi=pi, hg=hg, b=b, base_exp=base_exp, base_pv=base_pv):
                if pi == 0:
                    v.wait_ge(ld.h, ld.n)

                for c in range(NTILE):
                    v.wait_ge(s_exp.h, base_exp + c + 1)
                    for hi in range(2):
                        v_mk.inc(v.tensor_mul(probsT[c % 3][:, hi * 512:hi * 512 + 128],
                                              probsT[c % 3][:, hi * 512:hi * 512 + 128],
                                              mask_sb[:, c * 128:(c + 1) * 128]))
                if pi > 0:
                    v.wait_ge(pe_bc.h, 2 * pi)
                v.wait_ge(pv_h0.h, pi + 1)
                v_rr.inc(v.reciprocal(rr_sb[0][64:65, :], ctxp[0][0][64:65, 0:512]))
                v.wait_ge(pe_pv.h, base_pv + NTILE)
                v_rr.inc(v.reciprocal(rr_sb[1][64:65, :], ctxp[1][0][64:65, 0:512]))

            @block.vector
            def _(v, pi=pi, b=b, h0=h0, h1=h1):
                for hi, hh in enumerate((h0, h1)):
                    v.wait_ge(pe_bc.h, 2 * pi + hi + 1)
                    col = hh * TPC + b * 512
                    v_cx.inc(v.tensor_copy(rb_sb[hi][:, :], sps[1][:, hi * 512:(hi + 1) * 512]))
                    v.wait_ge(v_cx.h, v_cx.n)
                    v_cx.inc(v.tensor_mul(ctx_fm[0:64, col:col + 512], ctxp[hi][0][0:64, :],
                                          rb_sb[hi][0:64, :]))
                    v_cx.inc(v.tensor_mul(ctx_fm[64:128, col:col + 512],
                                          ctxp[hi][1][64:128, :], rb_sb[hi][64:128, :]))

        v_cx_total = v_cx.n
        ph4p.close()

        ph5p = ExitStack()
        op_ps = [ph5p.enter_context(nc.psum_tensor(f"op_ps{i}", [128, 512], F32)) for i in range(2)]
        bv_ps = ph5p.enter_context(nc.psum_tensor("bv_ps", [1, 512], F32))
        bc_ps = [ph5p.enter_context(nc.psum_tensor(f"bc_ps{i}", [128, 512], F32)) for i in range(2)]

        @block.sync
        def _(sync):
            for i in range(64):
                if i >= 3:
                    sync.wait_ge(v_co.h, i - 2)
                fg, dc = i // 16, i % 16
                ld_o[i % 3].inc(sync.dma_start(
                    out=wo_st[i % 3][:, :],
                    in_=wout_h[dc * 128:(dc + 1) * 128, fg * 512:(fg + 1) * 512]), 16)


        @block.tensor
        def _(t):
            t.wait_ge(v_cx.h, v_cx_total)
            t.wait_ge(v_ln.h, v_ln.n)
            for fg in range(4):
                for tt in range(NSLOT):
                    mm_i = fg * NSLOT + tt
                    if mm_i >= 2:
                        t.wait_ge(v_op.h, mm_i - 1)
                    for dc in range(16):
                        if tt == 0:
                            t.wait_ge(v_co.h, fg * 16 + dc + 1)
                        ins = t.matmul(
                            op_ps[tt % 2][:, :],
                            ctx_fm[:, dc * TPC + tt * 128:dc * TPC + (tt + 1) * 128],
                            wo_bf[:, dc * 512:(dc + 1) * 512],
                            start=(dc == 0), stop=(dc == 15))
                    pe_op.inc(ins)
                for dc in range(16):
                    if fg > 0 and dc == 0:
                        t.wait_ge(v_bv.h, fg)
                    pe_bv.inc(t.matmul(bv_ps[0:1, :],
                                       bqv_bf[:, dc:dc + 1],
                                       wo_bf[:, dc * 512:(dc + 1) * 512],
                                       start=(dc == 0), stop=(dc == 15)))

        @block.vector
        def _(v):
            for fg in range(4):
                for dc in range(16):
                    i = fg * 16 + dc
                    v.wait_ge(ld_o[i % 3].h, 16 * (i // 3 + 1))
                    if fg >= 1 and dc == 0:
                        v.wait_ge(pe_op.h, fg * NSLOT)
                        v.wait_ge(pe_bv.h, fg * 16)
                    v_co.inc(v.tensor_copy(wo_bf[:, dc * 512:(dc + 1) * 512],
                                           wo_st[i % 3][:, :]))
                for tt in range(NSLOT):
                    mm_i = fg * NSLOT + tt
                    v.wait_ge(pe_op.h, mm_i + 1)
                    v_op.inc(v.tensor_tensor(
                        x_res[:, tt * D + fg * 512:tt * D + fg * 512 + 512],
                        x_res[:, tt * D + fg * 512:tt * D + fg * 512 + 512],
                        op_ps[tt % 2][:, :], ALU.add))
                v.wait_ge(pe_bv.h, (fg + 1) * 16)
                if fg >= 1:
                    v.wait_ge(g_bs2.h, 32 + 16 * fg)
                v_bv.inc(v.tensor_copy(rr0[1][0:1, :], bv_ps[0:1, :]))
            v.wait_ge(v_op.h, v_op.n)
            v_fin.inc(v.engine_nop())

        @block.gpsimd
        def _(g):
            g_bs2.inc(g.dma_start(out=bsum_d.ap(), in_=bout_h), 16)
            g.wait_ge(g_bs2.h, 16)
            g_bs2.inc(g.dma_start(out=bsum_d.ap(), in_=bffn_h, accum_op=ALU.add), 16)
            g.wait_ge(g_bs2.h, 32)
            for fg in range(4):
                g.wait_ge(v_bv.h, fg + 1)
                g_bs2.inc(g.dma_start(out=bsum_d.ap()[fg * 512:(fg + 1) * 512],
                                      in_=rr0[1][0:1, :], accum_op=ALU.add), 16)

        @block.sync
        def _(sync):
            for k in range(4):
                sync.wait_ge(g_bs2.h, 32 + 16 * (k + 1))
                if k >= 1:
                    sync.wait_ge(pe_bb.h, k)
                g_bs.inc(sync.dma_start(out=rr0[0][0:1, :],
                                        in_=bsum_d.ap()[k * 512:(k + 1) * 512]), 16)

        @block.tensor
        def _(t):
            for k in range(4):
                t.wait_ge(g_bs.h, 16 * (k + 1))
                if k >= 2:
                    t.wait_ge(v_bb.h, 8 * (k - 1))
                pe_bb.inc(t.matmul(bc_ps[k % 2][:, :], ones_f[0:1, :],
                                   rr0[0][0:1, :], start=True, stop=True))

        @block.vector
        def _(v):
            for k in range(4):
                v.wait_ge(pe_bb.h, k + 1)
                for n in range(NSLOT):
                    v_bb.inc(v.tensor_tensor(
                        x_res[:, n * D + k * 512:n * D + k * 512 + 512],
                        x_res[:, n * D + k * 512:n * D + k * 512 + 512],
                        bc_ps[k % 2][:, :], ALU.add))

        @block.sync
        def _(sync):
            sync.wait_ge(v_fin.h, v_fin.n)
            for k in range(4):
                sync.wait_ge(v_bb.h, 8 * (k + 1))
                stv.inc(sync.dma_start(
                    out=bass.AP(out_t, k * 512, [[D, 128], [128 * D, 8], [1, 512]]),
                    in_=bass.AP(x_res, k * 512,
                                [[NSLOT * D, 128], [D, 8], [1, 512]])), 16)
            sync.wait_ge(stv.h, stv.n)

        ph5p.close()
        ph4s.close()

    return nc


_NC_CACHE = None


def _get_nc():
    global _NC_CACHE
    if _NC_CACHE is None:
        _NC_CACHE = build_nc()
    return _NC_CACHE


def _make_masks(j):
    m = np.zeros((NTILE, 128, 128), dtype=ml_dtypes.bfloat16)
    tri = (np.arange(128)[:, None] <= np.arange(128)[None, :]).astype(ml_dtypes.bfloat16)
    for c in range(NTILE):
        s0 = c // 8
        g0 = j + 8 * s0
        if c < g0:
            m[c, :, :] = 1.0
        elif c == g0:
            m[c, :, :] = tri
    return m


def kernel(L0_attn_x, L0_attn_W_qkv, L0_attn_b_qkv, L0_attn_W_out, L0_attn_b_out,
           L0_ffn_weight, L0_ffn_bias):
    x = np.ascontiguousarray(np.asarray(L0_attn_x, dtype=np.float32))
    wqkv = np.ascontiguousarray(np.asarray(L0_attn_W_qkv, dtype=np.float32))
    bqkv = np.ascontiguousarray(np.asarray(L0_attn_b_qkv, dtype=np.float32))
    wout = np.ascontiguousarray(np.asarray(L0_attn_W_out, dtype=np.float32))
    bout = np.ascontiguousarray(np.asarray(L0_attn_b_out, dtype=np.float32))
    wffn = np.ascontiguousarray(np.asarray(L0_ffn_weight, dtype=np.float32))
    bffn = np.ascontiguousarray(np.asarray(L0_ffn_bias, dtype=np.float32))

    ident = np.eye(128, dtype=ml_dtypes.bfloat16)

    in_maps = []
    for j in range(8):
        rows = []
        for b in range(B):
            for s in range(4):
                g = j + 8 * s
                rows.append(x[b, g * 128:(g + 1) * 128, :])
        xj = np.ascontiguousarray(np.concatenate(rows, axis=0))
        in_maps.append({
            "x": xj, "wqkv": wqkv, "bqkv": bqkv, "wout": wout, "bout": bout,
            "wffn": wffn, "bffn": bffn, "identb": ident, "masks": _make_masks(j),
        })

    nc = _get_nc()
    res = run_bass_kernel_spmd(nc, in_maps, core_ids=list(range(8)))

    out = np.empty((B, S, D), dtype=np.float32)
    for j in range(8):
        oj = res.results[j]["out"]
        for b in range(B):
            for s in range(4):
                g = j + 8 * s
                slot = b * 4 + s
                out[b, g * 128:(g + 1) * 128, :] = oj[slot * 128:(slot + 1) * 128, :]
    return out


# version 28
# speedup vs baseline: 1.0488x; 1.0091x over previous
import math
from contextlib import ExitStack

import numpy as np
import ml_dtypes

import concourse.bass as bass
import concourse.mybir as mybir
from concourse.bass_utils import run_bass_kernel_spmd

F32 = mybir.dt.float32
BF16 = mybir.dt.bfloat16
AF = mybir.ActivationFunctionType
ALU = mybir.AluOpType
AX = mybir.AxisListType

f32 = np.float32

B = 2
S = 4096
D = 2048
H = 16
HD = 128
NTILE = 32
NSLOT = 8
TPC = NSLOT * 128
SCALE = f32(1.0 / math.sqrt(HD))
SHIFT = f32(-12.0)

K_TILE_ELEMS = 128 * 16 * 128
V_TILE_ELEMS = 128 * 16 * 129
K_PART = NSLOT * K_TILE_ELEMS
V_PART = NSLOT * V_TILE_ELEMS
A_ELEMS = K_PART + V_PART


class _Ctr:
    def __init__(self, stack, nc, name):
        self.h = stack.enter_context(nc.semaphore(name))
        self.n = 0

    def inc(self, instr, k=1):
        instr.then_inc(self.h, k)
        self.n += k
        return instr


def build_nc():
    nc = bass.Bass("TRN2", target_bir_lowering=False)

    x_h = nc.dram_tensor("x", [TPC, D], F32, kind="ExternalInput").ap()
    wqkv_h = nc.dram_tensor("wqkv", [D, 3 * D], F32, kind="ExternalInput").ap()
    bqkv_h = nc.dram_tensor("bqkv", [3 * D], F32, kind="ExternalInput").ap()
    wout_h = nc.dram_tensor("wout", [D, D], F32, kind="ExternalInput").ap()
    bout_h = nc.dram_tensor("bout", [D], F32, kind="ExternalInput").ap()
    wffn_h = nc.dram_tensor("wffn", [D, D], F32, kind="ExternalInput").ap()
    bffn_h = nc.dram_tensor("bffn", [D], F32, kind="ExternalInput").ap()
    identb_h = nc.dram_tensor("identb", [128, 128], BF16, kind="ExternalInput").ap()
    masks_h = nc.dram_tensor("masks", [NTILE, 128, 128], BF16, kind="ExternalInput").ap()
    out_t = nc.dram_tensor("out", [TPC, D], F32, kind="ExternalOutput")
    out_h = out_t.ap()

    contrib_k = nc.dram_tensor("k_contrib", [K_PART], BF16, kind="Internal")
    contrib_v = nc.dram_tensor("v_contrib", [V_PART], BF16, kind="Internal")
    bsum_d = nc.dram_tensor("bsum_d", [D], F32, kind="Internal")
    rb_d = nc.dram_tensor("rb_d", [2, 512], F32, kind="Internal")
    agout_k = nc.dram_tensor("k_agout", [8, K_PART], BF16, kind="Internal",
                             addr_space="Shared")
    agout_v = nc.dram_tensor("v_agout", [8, V_PART], BF16, kind="Internal",
                             addr_space="Shared")

    with ExitStack() as st:
        e = st.enter_context

        x_res = e(nc.sbuf_tensor("x_res", [128, NSLOT * D], F32))
        q_fm = e(nc.sbuf_tensor("q_fm", [128, H * TPC], BF16))
        bq_sb = e(nc.sbuf_tensor("bq_sb", [128, 48], F32))
        bqv_bf = e(nc.sbuf_tensor("bqv_bf", [128, 16], BF16))
        identb = e(nc.sbuf_tensor("identb_s", [128, 128], BF16))
        ones_f = e(nc.sbuf_tensor("ones_f", [128, 128], F32))

        ld = _Ctr(st, nc, "ld")
        sld = _Ctr(st, nc, "sld")
        xld = [_Ctr(st, nc, f"xld{i}") for i in range(4)]
        s_ln = _Ctr(st, nc, "s_ln")
        v_ln = _Ctr(st, nc, "v_ln")
        pe_tr = _Ctr(st, nc, "pe_tr")
        v_xn = _Ctr(st, nc, "v_xn")
        ld_w = [_Ctr(st, nc, f"ld_w{i}") for i in range(2)]
        cast_p = [_Ctr(st, nc, f"cast_p{i}") for i in range(2)]
        ld_v = [_Ctr(st, nc, f"ld_v{i}") for i in range(2)]
        g_kb = [_Ctr(st, nc, f"g_kb{i}") for i in range(2)]
        cast_v = [_Ctr(st, nc, f"cast_v{i}") for i in range(2)]
        pe_qk = _Ctr(st, nc, "pe_qk")
        s_eqk = _Ctr(st, nc, "s_eqk")
        pe_v = _Ctr(st, nc, "pe_v")
        s_ev = _Ctr(st, nc, "s_ev")
        g_cw = _Ctr(st, nc, "g_cw")
        cc = _Ctr(st, nc, "cc")
        ld_f = [_Ctr(st, nc, f"ld_f{i}") for i in range(3)]
        v_f = _Ctr(st, nc, "v_f")
        s_f = _Ctr(st, nc, "s_f")
        pe_ft = _Ctr(st, nc, "pe_ft")
        v_fev = _Ctr(st, nc, "v_fev")
        pe_fm = _Ctr(st, nc, "pe_fm")
        v_fout = _Ctr(st, nc, "v_fout")
        kld = [_Ctr(st, nc, f"kld{i}") for i in range(4)]
        vld = [_Ctr(st, nc, f"vld{i}") for i in range(8)]
        pe_sc = _Ctr(st, nc, "pe_sc")
        s_exp = _Ctr(st, nc, "s_exp")
        v_mk = _Ctr(st, nc, "v_mk")
        pe_pv = _Ctr(st, nc, "pe_pv")
        pv_h0 = _Ctr(st, nc, "pv_h0")
        v_rr = _Ctr(st, nc, "v_rr")
        g_rr = _Ctr(st, nc, "g_rr")
        pe_bc = _Ctr(st, nc, "pe_bc")
        v_cx = _Ctr(st, nc, "v_cx")
        ld_o = [_Ctr(st, nc, f"ld_o{i}") for i in range(3)]
        v_co = _Ctr(st, nc, "v_co")
        pe_op = _Ctr(st, nc, "pe_op")
        v_op = _Ctr(st, nc, "v_op")
        pe_bv = _Ctr(st, nc, "pe_bv")
        v_bv = _Ctr(st, nc, "v_bv")
        pe_bb = _Ctr(st, nc, "pe_bb")
        v_bb = _Ctr(st, nc, "v_bb")
        g_bs = _Ctr(st, nc, "g_bs")
        g_bs2 = _Ctr(st, nc, "g_bs2")
        v_fin = _Ctr(st, nc, "v_fin")
        stv = _Ctr(st, nc, "stv")

        block = e(nc.Block())

        ph_xn = ExitStack()
        xn_fm = ph_xn.enter_context(nc.sbuf_tensor("xn_fm", [128, 16 * TPC], BF16))
        ph2ka = ExitStack()
        wstp0 = ph2ka.enter_context(nc.sbuf_tensor("wstp0", [128, 16 * 256], F32))
        ph2ka2 = ExitStack()
        wstp1 = ph2ka2.enter_context(nc.sbuf_tensor("wstp1", [128, 16 * 256], F32))
        wstp = [wstp0, wstp1]
        ph2kb = ExitStack()
        wsbp = [ph2kb.enter_context(nc.sbuf_tensor(f"wsbp{i}", [128, 16 * 256], BF16)) for i in range(2)]
        kbn = [ph2kb.enter_context(nc.sbuf_tensor(f"kbn{i}", [128, 512], BF16)) for i in range(2)]
        ph0b = ExitStack()
        bqT_st = ph0b.enter_context(nc.sbuf_tensor("bqT_st", [128, 128], F32))
        identf = ph0b.enter_context(nc.sbuf_tensor("identf", [128, 128], F32))

        def xpart(sync_or_sc, q):
            xld[q].inc(sync_or_sc.dma_start(
                out=x_res[:, q * 2 * D:(q + 1) * 2 * D].rearrange(
                    "p (n f) -> p n f", f=D),
                in_=x_h[q * 256:(q + 1) * 256].rearrange(
                    "(n p) f -> p n f", p=128)), 16)

        @block.sync
        def _(sync):
            ld.inc(sync.dma_start(out=identb[:, :], in_=identb_h[:, :]), 16)
            xpart(sync, 0)
            xpart(sync, 2)

        @block.vector
        def _(v):
            v_ln.inc(v.memset(bqT_st[:, :], 0.0))
            v.wait_ge(ld.h, 16)
            v_ln.inc(v.tensor_copy(identf[:, :], identb[:, :]))

        @block.gpsimd
        def _(g):
            g_bs2.inc(g.dma_start(out=bsum_d.ap(), in_=bout_h), 16)
            g.wait_ge(g_bs2.h, 16)
            g_bs2.inc(g.dma_start(out=bsum_d.ap(), in_=bffn_h, accum_op=ALU.add), 16)

        @block.scalar
        def _(sc):
            xpart(sc, 1)
            xpart(sc, 3)
            sc.wait_ge(v_ln.h, 1)
            sld.inc(sc.dma_start(out=bqT_st[0:48, :],
                                 in_=bqkv_h.rearrange("(f p) -> f p", p=128)), 16)

        ld_total = ld.n

        ph1 = ExitStack()
        xn_tm = [ph1.enter_context(nc.sbuf_tensor(f"xn_tm{i}", [128, D], BF16)) for i in range(2)]
        stat = ph1.enter_context(nc.sbuf_tensor("stat", [128, 80], F32))
        pt_a = ph1.enter_context(nc.psum_tensor("pt_a", [128, 512], BF16))
        pt_b = ph1.enter_context(nc.psum_tensor("pt_b", [128, 512], BF16))
        bqp = ph1.enter_context(nc.psum_tensor("bqp", [128, 128], F32))

        v_base = v_ln.n
        s_base = s_ln.n
        presqrt_v = [v_base + 11 * hf + 8 for hf in range(2)]
        rec_v = [v_base + 11 * hf + 11 for hf in range(2)]
        sq_done = [s_base + 4, s_base + 9]
        sqrt_done = [s_base + 5, s_base + 10]
        xn_done = [s_base + 11 + n for n in range(NSLOT)]
        after_sq = [s_base + m + 1 + (1 if m >= 4 else 0) for m in range(NSLOT)]

        @block.vector
        def _(v):
            for hf in range(2):
                c0 = 4 * hf
                for n in range(c0, c0 + 4):
                    v.wait_ge(xld[n // 2].h, 16)
                    v_ln.inc(v.reduce_sum(stat[:, n:n + 1], x_res[:, n * D:(n + 1) * D],
                                          axis=AX.X))
                v.wait_ge(s_ln.h, sq_done[hf])
                v.wait_ge(v_ln.h, v_base + 11 * hf + 4)
                v_ln.inc(v.tensor_scalar(stat[:, 16 + c0:20 + c0], stat[:, c0:4 + c0],
                                         float(1.0 / D), None, ALU.mult))
                v_ln.inc(v.tensor_scalar(stat[:, 24 + c0:28 + c0], stat[:, 8 + c0:12 + c0],
                                         float(1.0 / D), None, ALU.mult))
                v.wait_ge(v_ln.h, v_ln.n)
                v_ln.inc(v.tensor_mul(stat[:, 32 + c0:36 + c0], stat[:, 16 + c0:20 + c0],
                                      stat[:, 16 + c0:20 + c0]))
                v.wait_ge(v_ln.h, v_ln.n)
                v_ln.inc(v.tensor_sub(stat[:, 24 + c0:28 + c0], stat[:, 24 + c0:28 + c0],
                                      stat[:, 32 + c0:36 + c0]))
                v.wait_ge(s_ln.h, sqrt_done[hf])
                v_ln.inc(v.reciprocal(stat[:, 48 + c0:52 + c0], stat[:, 40 + c0:44 + c0]))
                v_ln.inc(v.tensor_scalar(stat[:, 64 + c0:68 + c0], stat[:, 16 + c0:20 + c0],
                                         -1.0, None, ALU.mult))
                v.wait_ge(v_ln.h, v_ln.n)
                v_ln.inc(v.tensor_mul(stat[:, 56 + c0:60 + c0], stat[:, 64 + c0:68 + c0],
                                      stat[:, 48 + c0:52 + c0]))

        @block.scalar
        def _(sc):
            for hf in range(2):
                c0 = 4 * hf
                for n in range(c0, c0 + 4):
                    sc.wait_ge(xld[n // 2].h, 16)
                    if n >= 1:
                        sc.wait_ge(s_ln.h, after_sq[n - 1])
                    s_ln.inc(sc.activation(q_fm[:, 0:D], x_res[:, n * D:(n + 1) * D],
                                           AF.Square, accum_out=stat[:, 8 + n:9 + n]))
                sc.wait_ge(v_ln.h, presqrt_v[hf])
                s_ln.inc(sc.activation(stat[:, 40 + c0:44 + c0], stat[:, 24 + c0:28 + c0],
                                       AF.Sqrt, bias=f32(1e-5)))
            for n in range(NSLOT):
                if n == 0:
                    sc.wait_ge(v_ln.h, rec_v[0])
                elif n == 4:
                    sc.wait_ge(v_ln.h, rec_v[1])
                if n >= 2:
                    sc.wait_ge(pe_tr.h, 16 * (n - 1))
                s_ln.inc(sc.activation(xn_tm[n % 2][:, :],
                                       x_res[:, n * D:(n + 1) * D], AF.Identity,
                                       bias=stat[:, 56 + n:57 + n],
                                       scale=stat[:, 48 + n:49 + n]))

        @block.tensor
        def _(t):
            t.wait_ge(ld.h, ld_total)
            for n in range(NSLOT):
                t.wait_ge(s_ln.h, xn_done[n])
                for g in range(4):
                    gi = n * 4 + g
                    pt = pt_a if gi % 2 == 0 else pt_b
                    if gi >= 2:
                        t.wait_ge(v_xn.h, gi - 1)
                    for i in range(4):
                        phi = g * 4 + i
                        pe_tr.inc(t.transpose(pt[:, i * 128:(i + 1) * 128],
                                              xn_tm[n % 2][:, phi * 128:(phi + 1) * 128],
                                              identb[:, :]))

        @block.vector
        def _(v):
            for n in range(NSLOT):
                for g in range(4):
                    gi = n * 4 + g
                    pt = pt_a if gi % 2 == 0 else pt_b
                    v.wait_ge(pe_tr.h, 4 * (gi + 1))
                    dst = bass.AP(xn_fm, (g * 4) * TPC + n * 128,
                                  [[16 * TPC, 128], [TPC, 4], [1, 128]])
                    v_xn.inc(v.tensor_copy(dst, pt[:, :].rearrange("p (i t) -> p i t", t=128)))

        @block.tensor
        def _(t):
            t.wait_ge(sld.h, 16)
            t.wait_ge(v_ln.h, 2)
            pe_tr.inc(t.transpose(bqp[:, :], bqT_st[:, :], identf[:, :]))
        pe_bq_done = pe_tr.n

        @block.vector
        def _(v):
            v.wait_ge(pe_tr.h, pe_bq_done)
            v_ln.inc(v.tensor_copy(bq_sb[:, :], bqp[:, :48]))
        v_bqc = v_ln.n

        v_xn_total = v_xn.n
        ph1.close()
        ph0b.close()

        @block.vector
        def _(v):
            v_ln.inc(v.memset(ones_f[:, :], 1.0))
            v.wait_ge(v_ln.h, v_bqc)
            v_ln.inc(v.tensor_copy(bqv_bf[:, :], bass.AP(bq_sb, 2, [[48, 128], [3, 16]])))

        ph2 = ExitStack()
        qk_ps = [ph2.enter_context(nc.psum_tensor(f"qk_ps{i}", [128, 512], F32)) for i in range(2)]

        @block.sync
        def _(sync):
            for i in range(16):
                if i >= 2:
                    sync.wait_ge(cast_p[(i - 2) % 2].h, (i - 2) // 2 + 1)
                src = wqkv_h.rearrange("(cc p) f -> p cc f", p=128)[
                    :, :, (3 * i) * 128:(3 * i) * 128 + 256]
                ld_w[i % 2].inc(sync.dma_start(
                    out=wstp[i % 2][:, :].rearrange("p (cc f) -> p cc f", f=256),
                    in_=src), 16)

        @block.vector
        def _(v):
            for i in range(16):
                v.wait_ge(ld_w[i % 2].h, 16 * (i // 2 + 1))
                if i >= 2:
                    v.wait_ge(pe_qk.h, 4 * (i - 1))
                cast_p[i % 2].inc(v.tensor_copy(wsbp[i % 2][:, :], wstp[i % 2][:, :]))

        @block.tensor
        def _(t):
            t.wait_ge(v_ln.h, v_bqc)
            for i in range(16):
                t.wait_ge(cast_p[i % 2].h, i // 2 + 1)
                for kq in range(2):
                    off = 128 * (1 - kq)
                    for tb in range(2):
                        if i == 0 and kq == 0:
                            t.wait_ge(v_xn.h, 16 if tb == 0 else v_xn_total)
                        if pe_qk.n >= 2:
                            t.wait_ge(s_eqk.h, pe_qk.n - 1)
                        for cj in range(16):
                            ins = t.matmul(
                                qk_ps[pe_qk.n % 2][:, :],
                                wsbp[i % 2][:, cj * 256 + off:cj * 256 + off + 128],
                                xn_fm[:, cj * TPC + tb * 512:cj * TPC + tb * 512 + 512],
                                start=(cj == 0), stop=(cj == 15))
                        pe_qk.inc(ins)

        @block.scalar
        def _(sc):
            for i in range(16):
                for kq in range(2):
                    F = 3 * i + (1 - kq)
                    for tb in range(2):
                        m = s_eqk.n
                        sc.wait_ge(pe_qk.h, m + 1)
                        if kq == 0:
                            j = 2 * i + tb
                            if j >= 2:
                                sc.wait_ge(g_kb[j % 2].h, 16 * (j // 2))
                            s_eqk.inc(sc.activation(
                                kbn[j % 2][:, :], qk_ps[m % 2][:, :],
                                AF.Identity, bias=bq_sb[:, F:F + 1], scale=f32(1.0)))
                            sc.wait_ge(s_eqk.h, m + 1)
                            dst = bass.AP(contrib_k, (tb * 4) * K_TILE_ELEMS + i * 128,
                                          [[2048, 128], [K_TILE_ELEMS, 4], [1, 128]])
                            g_kb[j % 2].inc(sc.dma_start(
                                out=dst,
                                in_=kbn[j % 2][:, :].rearrange("p (s t) -> p s t", t=128)), 16)
                        else:
                            s_eqk.inc(sc.activation(
                                q_fm[:, i * TPC + tb * 512:i * TPC + tb * 512 + 512],
                                qk_ps[m % 2][:, :], AF.Identity,
                                bias=bq_sb[:, F:F + 1], scale=f32(1.0)))

        n_kq_evac = s_eqk.n
        n_q_evac = n_kq_evac

        @block.gpsimd
        def _(g):
            g.wait_ge(g_kb[0].h, g_kb[0].n)
            g.wait_ge(g_kb[1].h, g_kb[1].n)
            cc.inc(g.collective_compute(
                "AllGather", ALU.bypass, replica_groups=[list(range(8))],
                ins=[contrib_k.ap().opt()], outs=[agout_k.ap().opt()]))

        n_kb = (g_kb[0].n, g_kb[1].n)
        ph2kb.close()
        ph2ka2.close()

        ph2v = ExitStack()
        v_stage = ph2v.enter_context(nc.sbuf_tensor("v_stage", [128, NSLOT * H * 129], BF16))
        wsb_v = ph2v.enter_context(nc.sbuf_tensor("wsb_v", [128, 16 * 512], BF16))

        @block.vector
        def _(v):
            v.wait_ge(g_kb[0].h, n_kb[0])
            v.wait_ge(g_kb[1].h, n_kb[1])
            v.wait_ge(pe_qk.h, 64)
            v.wait_ge(cast_p[1].h, 8)
            v_ln.inc(v.memset(bass.AP(v_stage, 64,
                              [[NSLOT * H * 129, 128], [129, NSLOT * H], [1, 1]]), 1.0))

        v_ones = v_ln.n

        @block.sync
        def _(sync):
            for i in range(16, 32):
                if i < 18:
                    sync.wait_ge(cast_p[0].h, 8)
                else:
                    sync.wait_ge(cast_v[(i - 2) % 2].h, (i - 18) // 2 + 1)
                F = 3 * (i - 16) + 2
                sl = (i % 2) * (16 * 128)
                src = wqkv_h.rearrange("(cc p) f -> p cc f", p=128)[:, :, F * 128:(F + 1) * 128]
                ld_v[i % 2].inc(sync.dma_start(
                    out=wstp0[:, sl:sl + 16 * 128].rearrange("p (cc f) -> p cc f", f=128),
                    in_=src), 16)

        @block.vector
        def _(v):
            for i in range(16, 32):
                v.wait_ge(ld_v[i % 2].h, 16 * ((i - 16) // 2 + 1))
                G, hh = (i - 16) // 4, (i - 16) % 4
                if i < 18:
                    v.wait_ge(g_kb[0].h, n_kb[0])
                    v.wait_ge(g_kb[1].h, n_kb[1])
                    v.wait_ge(pe_qk.h, 64)
                    v.wait_ge(cast_p[1].h, 8)
                if G >= 1 and hh == 0:
                    v.wait_ge(pe_v.h, 8 * G)
                sl = (i % 2) * (16 * 128)
                dst = bass.AP(wsb_v, hh * 128,
                              [[16 * 512, 128], [512, 16], [1, 128]])
                cast_v[i % 2].inc(v.tensor_copy(
                    dst, wstp0[:, sl:sl + 16 * 128].rearrange("p (cc f) -> p cc f", f=128)))

        @block.tensor
        def _(t):
            for G in range(4):
                t.wait_ge(cast_v[0].h, 2 * (G + 1))
                t.wait_ge(cast_v[1].h, 2 * (G + 1))
                for n in range(NSLOT):
                    if pe_v.n >= 2:
                        t.wait_ge(s_ev.h, pe_v.n - 1)
                    else:
                        t.wait_ge(s_eqk.h, n_kq_evac)
                    for cj in range(16):
                        ins = t.matmul(
                            qk_ps[pe_v.n % 2][:, :],
                            xn_fm[:, cj * TPC + n * 128:cj * TPC + (n + 1) * 128],
                            wsb_v[:, cj * 512:(cj + 1) * 512],
                            start=(cj == 0), stop=(cj == 15))
                    pe_v.inc(ins)

        @block.scalar
        def _(sc):
            sc.wait_ge(v_ln.h, v_ones)
            for G in range(4):
                for n in range(NSLOT):
                    sc.wait_ge(pe_v.h, s_ev.n + 1)
                    dst = bass.AP(v_stage, n * (H * 129) + (4 * G) * 129,
                                  [[NSLOT * H * 129, 128], [129, 4], [65, 2], [1, 64]])
                    s_ev.inc(sc.copy(dst, qk_ps[s_ev.n % 2][:, :].rearrange(
                        "p (hh s d) -> p hh s d", s=2, d=64)))

        n_v_evac = s_ev.n

        @block.scalar
        def _(sc):
            sc.wait_ge(s_ev.h, n_v_evac)
            for n in range(NSLOT):
                dst = bass.AP(contrib_v, n * V_TILE_ELEMS, [[2064, 128], [1, 2064]])
                g_cw.inc(sc.dma_start(out=dst, in_=v_stage[:, n * 2064:(n + 1) * 2064]), 16)

        @block.gpsimd
        def _(g):
            g.wait_ge(g_cw.h, g_cw.n)
            cc.inc(g.collective_compute(
                "AllGather", ALU.bypass, replica_groups=[list(range(8))],
                ins=[contrib_v.ap().opt()], outs=[agout_v.ap().opt()]))

        n_cv = g_cw.n
        ph2v.close()
        ph2ka.close()

        ph3 = ExitStack()
        ffn_st = [ph3.enter_context(nc.sbuf_tensor(f"ffn_st{i}", [128, D], F32)) for i in range(3)]
        sgn_bf = [ph3.enter_context(nc.sbuf_tensor(f"sgn_bf{i}", [128, D], BF16)) for i in range(3)]
        alph = ph3.enter_context(nc.sbuf_tensor("alph", [128, 16], F32))
        bwT_g = ph3.enter_context(nc.sbuf_tensor("bwT_g", [128, 16 * 512], BF16))
        ptt = [ph3.enter_context(nc.psum_tensor(f"ptt{i}", [128, 512], BF16)) for i in range(2)]
        ffn_ps = [ph3.enter_context(nc.psum_tensor(f"ffn_ps{i}", [128, 512], F32)) for i in range(2)]

        @block.sync
        def _(sync):
            for oc in range(16):
                if oc < 2:
                    sync.wait_ge(cast_v[0].h, 8)
                    sync.wait_ge(cast_v[1].h, 8)
                elif oc == 2:
                    sync.wait_ge(g_cw.h, n_cv)
                if oc >= 3:
                    sync.wait_ge(s_f.h, oc - 2)
                    sync.wait_ge(v_f.h, 3 * (oc - 2))
                ld_f[oc % 3].inc(sync.dma_start(out=ffn_st[oc % 3][:, :],
                                        in_=wffn_h[oc * 128:(oc + 1) * 128, :]), 16)


        @block.scalar
        def _(sc):
            for oc in range(16):
                sc.wait_ge(ld_f[oc % 3].h, 16 * (oc // 3 + 1))
                if oc == 0:
                    sc.wait_ge(g_cw.h, n_cv)
                if oc >= 3:
                    sc.wait_ge(pe_ft.h, 16 * (oc - 2))
                    sc.wait_ge(v_f.h, 3 * (oc - 2))
                s_f.inc(sc.sign(sgn_bf[oc % 3][:, :], ffn_st[oc % 3][:, :]))

        @block.tensor
        def _(t):
            t.wait_ge(s_eqk.h, n_q_evac)
            t.wait_ge(s_ev.h, n_v_evac)
            ng = 0
            for og in range(4):
                for k in range(4):
                    oc = og * 4 + k
                    t.wait_ge(v_f.h, 3 * (oc + 1))
                    for g in range(4):
                        ng += 1
                        if ng > 2:
                            t.wait_ge(v_fev.h, ng - 2)
                        for i in range(4):
                            ib = g * 4 + i
                            pe_ft.inc(t.transpose(ptt[ng % 2][:, i * 128:(i + 1) * 128],
                                                  sgn_bf[oc % 3][:, ib * 128:(ib + 1) * 128],
                                                  identb[:, :]))
                t.wait_ge(v_fev.h, ng)
                for tt in range(NSLOT):
                    mm_i = og * NSLOT + tt
                    if mm_i >= 2:
                        t.wait_ge(v_fout.h, mm_i - 1)
                    for ic in range(16):
                        ins = t.matmul(
                            ffn_ps[tt % 2][:, :],
                            xn_fm[:, ic * TPC + tt * 128:ic * TPC + (tt + 1) * 128],
                            bwT_g[:, ic * 512:(ic + 1) * 512],
                            start=(ic == 0), stop=(ic == 15))
                    pe_fm.inc(ins)

        @block.vector
        def _(v):
            v.wait_ge(g_cw.h, n_cv)

            def _stats(oc):
                v.wait_ge(ld_f[oc % 3].h, 16 * (oc // 3 + 1))
                v_f.inc(v.reduce_sum(alph[:, oc:oc + 1], ffn_st[oc % 3][:, :],
                                     axis=AX.X, apply_absolute_value=True))
                v.wait_ge(v_f.h, v_f.n)
                v_f.inc(v.tensor_scalar(alph[:, oc:oc + 1], alph[:, oc:oc + 1],
                                        float(1.0 / D), None, ALU.mult))
                v.wait_ge(v_f.h, v_f.n)
                v.wait_ge(s_f.h, oc + 1)
                v_f.inc(v.tensor_scalar(sgn_bf[oc % 3][:, :], sgn_bf[oc % 3][:, :],
                                        alph[:, oc:oc + 1], None, ALU.mult))

            for oc in range(3):
                _stats(oc)
            ng = 0
            for og in range(4):
                for k in range(4):
                    oc = og * 4 + k
                    for g in range(4):
                        ng += 1
                        v.wait_ge(pe_ft.h, 4 * ng)
                        if og > 0 and k == 0 and g == 0:
                            v.wait_ge(pe_fm.h, og * NSLOT)
                        dst = bass.AP(bwT_g, (g * 4) * 512 + k * 128,
                                      [[16 * 512, 128], [512, 4], [1, 128]])
                        v_fev.inc(v.tensor_copy(dst, ptt[ng % 2][:, :].rearrange(
                            "p (i o) -> p i o", o=128)))
                    if oc + 3 < 16:
                        _stats(oc + 3)
                for tt in range(NSLOT):
                    mm_i = og * NSLOT + tt
                    v.wait_ge(pe_fm.h, mm_i + 1)
                    v_fout.inc(v.tensor_tensor(
                        x_res[:, tt * D + og * 512:tt * D + og * 512 + 512],
                        x_res[:, tt * D + og * 512:tt * D + og * 512 + 512],
                        ffn_ps[tt % 2][:, :], ALU.add))

        n_fout = 32
        n_fev = 64
        n_sf = 16
        ph3.close()
        ph2.close()
        ph_xn.close()

        ph4s = ExitStack()
        ctx_fm = ph4s.enter_context(nc.sbuf_tensor("ctx_fm", [128, H * TPC], BF16))
        mask_sb = ph4s.enter_context(nc.sbuf_tensor("mask_sb", [128, NTILE * 128], BF16))
        kpair = [ph4s.enter_context(nc.sbuf_tensor(f"kpair{i}", [128, 256], BF16)) for i in range(4)]
        vaug = [ph4s.enter_context(nc.sbuf_tensor(f"vaug{i}", [128, 258], BF16)) for i in range(8)]
        probsT = [ph4s.enter_context(nc.sbuf_tensor(f"probsT{i}", [128, 1024], BF16)) for i in range(3)]
        rr_sb = [ph4s.enter_context(nc.sbuf_tensor(f"rr_sb{i}", [128, 512], F32)) for i in range(2)]
        rr0 = [ph4s.enter_context(nc.sbuf_tensor(f"rr0_{i}", [1, 512], F32)) for i in range(2)]
        rb_sb = [ph4s.enter_context(nc.sbuf_tensor(f"rb_sb{i}", [128, 512], F32)) for i in range(2)]
        wo_st = [ph4s.enter_context(nc.sbuf_tensor(f"wo_st{i}", [128, 512], F32)) for i in range(3)]
        wo_bf = ph4s.enter_context(nc.sbuf_tensor("wo_bf", [128, 16 * 512], BF16))

        ph4p = ExitStack()
        sps = [ph4p.enter_context(nc.psum_tensor(f"sps{i}", [128, 1024], F32)) for i in range(2)]
        ctxp = [[ph4p.enter_context(nc.psum_tensor(f"ctxp{hh}_{i}", [128, 512], F32))
                 for i in range(2)] for hh in range(2)]

        def n_of(c):
            return 512 - (c // 8) * 128

        for pi in range(16):
            hg, b = pi // 2, pi % 2
            h0, h1 = 2 * hg, 2 * hg + 1
            base_sc = pe_sc.n
            base_exp = s_exp.n
            base_mk = v_mk.n
            base_pv = pe_pv.n
            base_cx = v_cx.n
            base_rr_t = 2 * (pi + 1)

            @block.sync
            def _(sync, pi=pi, hg=hg, b=b, base_sc=base_sc):
                if pi == 0:
                    sync.wait_ge(cc.h, 2)
                    sync.wait_ge(s_eqk.h, n_q_evac)
                    sync.wait_ge(s_ev.h, n_v_evac)
                    sync.wait_ge(v_fout.h, n_fout)
                    sync.wait_ge(v_fev.h, n_fev)
                    sync.wait_ge(s_f.h, n_sf)
                    for q in range(4):
                        ld.inc(sync.dma_start(
                            out=mask_sb[:, q * 1024:(q + 1) * 1024].rearrange(
                                "p (c q) -> p c q", q=128),
                            in_=masks_h[q * 8:(q + 1) * 8].transpose([1, 0, 2])), 16)
                def kload(cg):
                    pj, c = cg // NTILE, cg % NTILE
                    hgj, bj = pj // 2, pj % 2
                    tgt = 2 * (cg - 3)
                    if tgt > 0:
                        sync.wait_ge(pe_sc.h, tgt)
                    owner = c % 8
                    tile = bj * 4 + c // 8
                    koff = owner * K_PART + tile * K_TILE_ELEMS + (2 * hgj) * 128
                    kld[c % 4].inc(sync.dma_start(
                        out=kpair[c % 4][:, :],
                        in_=bass.AP(agout_k, koff, [[2048, 128], [1, 256]])), 16)

                if pi == 0:
                    for c in range(4):
                        kload(c)
                for c in range(4, NTILE + 4):
                    cg = pi * NTILE + c
                    if cg < 16 * NTILE:
                        kload(cg)

            def _pv(t, cp, pi=pi, base_mk=base_mk):
                nc_p = n_of(cp)
                cc0 = (cp // 8) * 128
                cg = pi * NTILE + cp
                t.wait_ge(vld[cg % 8].h, 16 * (cg // 8 + 1))
                t.wait_ge(v_mk.h, base_mk + 2 * (cp + 1))
                for hi in range(2):
                    t.matmul(ctxp[hi][0][0:65, cc0:512],
                             vaug[cp % 8][:, hi * 129:hi * 129 + 65],
                             probsT[cp % 3][:, hi * 512:hi * 512 + nc_p],
                             start=(cp == 0), stop=(cp == NTILE - 1))
                    ins = t.matmul(ctxp[hi][1][64:128, cc0:512],
                                   vaug[cp % 8][:, hi * 129 + 65:hi * 129 + 129],
                                   probsT[cp % 3][:, hi * 512:hi * 512 + nc_p],
                                   start=(cp == 0), stop=(cp == NTILE - 1))
                    if cp == NTILE - 1 and hi == 0:
                        pv_h0.inc(ins)
                pe_pv.inc(ins)

            @block.tensor
            def _(t, pi=pi, b=b, h0=h0, h1=h1, base_exp=base_exp,
                  base_cx=base_cx, base_rr_t=base_rr_t, _pv=_pv):
                if pi == 0:
                    t.wait_ge(v_cx.h, base_cx)
                    t.wait_ge(v_fout.h, n_fout)
                    t.wait_ge(v_fev.h, n_fev)
                    t.wait_ge(v_ln.h, v_ln.n)
                t.wait_ge(s_exp.h, base_exp)
                for c in range(NTILE):
                    nc_w = n_of(c)
                    cc0 = (c // 8) * 128
                    cgk = pi * NTILE + c
                    t.wait_ge(kld[cgk % 4].h, 16 * (cgk // 4 + 1))
                    if c >= 2:
                        t.wait_ge(s_exp.h, base_exp + c - 1)
                    if pi > 0 and c == 1:
                        t.wait_ge(v_cx.h, base_cx - 2)
                    for hi, hh in enumerate((h0, h1)):
                        qo = hh * TPC + b * 512 + cc0
                        pe_sc.inc(t.matmul(sps[c % 2][:, hi * 512:hi * 512 + nc_w],
                                           kpair[c % 4][:, hi * 128:(hi + 1) * 128],
                                           q_fm[:, qo:qo + nc_w], start=True, stop=True))
                    if c >= 2:
                        if pi > 0 and c == 2:
                            t.wait_ge(v_cx.h, base_cx)
                        _pv(t, c - 2)
                _pv(t, NTILE - 2)
                _pv(t, NTILE - 1)
                for hi in range(2):
                    t.wait_ge(v_rr.h, 2 * pi + hi + 1)
                    pe_bc.inc(t.matmul(sps[1][:, hi * 512:(hi + 1) * 512],
                                       ones_f[64:65, :], rr_sb[hi][64:65, :],
                                       start=True, stop=True))

            @block.scalar
            def _(sc, pi=pi, hg=hg, b=b, base_sc=base_sc, base_pv=base_pv):
                if pi == 0:
                    sc.wait_ge(cc.h, 2)
                    sc.wait_ge(v_fout.h, n_fout)
                    sc.wait_ge(v_fev.h, n_fev)
                def vload(cg):
                    pj, c = cg // NTILE, cg % NTILE
                    hgj, bj = pj // 2, pj % 2
                    tgt = cg - 7
                    if tgt > 0:
                        sc.wait_ge(pe_pv.h, tgt)
                    owner = c % 8
                    tile = bj * 4 + c // 8
                    voff = owner * V_PART + tile * V_TILE_ELEMS + (2 * hgj) * 129
                    vld[c % 8].inc(sc.dma_start(
                        out=vaug[c % 8][:, :],
                        in_=bass.AP(agout_v, voff, [[2064, 128], [1, 258]])), 16)

                if pi == 0:
                    for c in range(4):
                        vload(c)
                for c in range(NTILE):
                    nc_w = n_of(c)
                    cg = pi * NTILE + c + 4
                    if cg < 16 * NTILE:
                        vload(cg)
                    sc.wait_ge(pe_sc.h, base_sc + 2 * (c + 1))
                    if c >= 3:
                        sc.wait_ge(pe_pv.h, base_pv + c - 2)
                    src = bass.AP(sps[c % 2], 0, [[1024, 128], [512, 2], [1, nc_w]])
                    dst = bass.AP(probsT[c % 3], 0, [[1024, 128], [512, 2], [1, nc_w]])
                    s_exp.inc(sc.activation(dst, src, AF.Exp, bias=SHIFT, scale=SCALE))

            @block.vector
            def _(v, pi=pi, hg=hg, b=b, base_exp=base_exp, base_pv=base_pv):
                if pi == 0:
                    v.wait_ge(ld.h, ld.n)

                for c in range(NTILE):
                    v.wait_ge(s_exp.h, base_exp + c + 1)
                    for hi in range(2):
                        v_mk.inc(v.tensor_mul(probsT[c % 3][:, hi * 512:hi * 512 + 128],
                                              probsT[c % 3][:, hi * 512:hi * 512 + 128],
                                              mask_sb[:, c * 128:(c + 1) * 128]))
                if pi > 0:
                    v.wait_ge(pe_bc.h, 2 * pi)
                v.wait_ge(pv_h0.h, pi + 1)
                v_rr.inc(v.reciprocal(rr_sb[0][64:65, :], ctxp[0][0][64:65, 0:512]))
                v.wait_ge(pe_pv.h, base_pv + NTILE)
                v_rr.inc(v.reciprocal(rr_sb[1][64:65, :], ctxp[1][0][64:65, 0:512]))

            @block.vector
            def _(v, pi=pi, b=b, h0=h0, h1=h1):
                for hi, hh in enumerate((h0, h1)):
                    v.wait_ge(pe_bc.h, 2 * pi + hi + 1)
                    col = hh * TPC + b * 512
                    v_cx.inc(v.tensor_copy(rb_sb[hi][:, :], sps[1][:, hi * 512:(hi + 1) * 512]))
                    v.wait_ge(v_cx.h, v_cx.n)
                    v_cx.inc(v.tensor_mul(ctx_fm[0:64, col:col + 512], ctxp[hi][0][0:64, :],
                                          rb_sb[hi][0:64, :]))
                    v_cx.inc(v.tensor_mul(ctx_fm[64:128, col:col + 512],
                                          ctxp[hi][1][64:128, :], rb_sb[hi][64:128, :]))

        v_cx_total = v_cx.n
        ph4p.close()

        ph5p = ExitStack()
        op_ps = [ph5p.enter_context(nc.psum_tensor(f"op_ps{i}", [128, 512], F32)) for i in range(2)]
        bv_ps = ph5p.enter_context(nc.psum_tensor("bv_ps", [1, 512], F32))
        bc_ps = [ph5p.enter_context(nc.psum_tensor(f"bc_ps{i}", [128, 512], F32)) for i in range(2)]

        @block.sync
        def _(sync):
            for i in range(64):
                if i >= 3:
                    sync.wait_ge(v_co.h, i - 2)
                fg, dc = i // 16, i % 16
                ld_o[i % 3].inc(sync.dma_start(
                    out=wo_st[i % 3][:, :],
                    in_=wout_h[dc * 128:(dc + 1) * 128, fg * 512:(fg + 1) * 512]), 16)


        @block.tensor
        def _(t):
            t.wait_ge(v_cx.h, v_cx_total)
            t.wait_ge(v_ln.h, v_ln.n)
            for fg in range(4):
                for tt in range(NSLOT):
                    mm_i = fg * NSLOT + tt
                    if mm_i >= 2:
                        t.wait_ge(v_op.h, mm_i - 1)
                    for dc in range(16):
                        if tt == 0:
                            t.wait_ge(v_co.h, fg * 16 + dc + 1)
                        ins = t.matmul(
                            op_ps[tt % 2][:, :],
                            ctx_fm[:, dc * TPC + tt * 128:dc * TPC + (tt + 1) * 128],
                            wo_bf[:, dc * 512:(dc + 1) * 512],
                            start=(dc == 0), stop=(dc == 15))
                    pe_op.inc(ins)
                for dc in range(16):
                    if fg > 0 and dc == 0:
                        t.wait_ge(v_bv.h, fg)
                    pe_bv.inc(t.matmul(bv_ps[0:1, :],
                                       bqv_bf[:, dc:dc + 1],
                                       wo_bf[:, dc * 512:(dc + 1) * 512],
                                       start=(dc == 0), stop=(dc == 15)))

        @block.vector
        def _(v):
            for fg in range(4):
                for dc in range(16):
                    i = fg * 16 + dc
                    v.wait_ge(ld_o[i % 3].h, 16 * (i // 3 + 1))
                    if fg >= 1 and dc == 0:
                        v.wait_ge(pe_op.h, fg * NSLOT)
                        v.wait_ge(pe_bv.h, fg * 16)
                    v_co.inc(v.tensor_copy(wo_bf[:, dc * 512:(dc + 1) * 512],
                                           wo_st[i % 3][:, :]))
                for tt in range(NSLOT):
                    mm_i = fg * NSLOT + tt
                    v.wait_ge(pe_op.h, mm_i + 1)
                    v_op.inc(v.tensor_tensor(
                        x_res[:, tt * D + fg * 512:tt * D + fg * 512 + 512],
                        x_res[:, tt * D + fg * 512:tt * D + fg * 512 + 512],
                        op_ps[tt % 2][:, :], ALU.add))
                v.wait_ge(pe_bv.h, (fg + 1) * 16)
                if fg >= 1:
                    v.wait_ge(g_bs2.h, 32 + 16 * fg)
                v_bv.inc(v.tensor_copy(rr0[1][0:1, :], bv_ps[0:1, :]))
            v.wait_ge(v_op.h, v_op.n)
            v_fin.inc(v.engine_nop())

        @block.gpsimd
        def _(g):
            g.wait_ge(g_bs2.h, 32)
            for fg in range(4):
                g.wait_ge(v_bv.h, fg + 1)
                g_bs2.inc(g.dma_start(out=bsum_d.ap()[fg * 512:(fg + 1) * 512],
                                      in_=rr0[1][0:1, :], accum_op=ALU.add), 16)

        @block.sync
        def _(sync):
            for k in range(4):
                sync.wait_ge(g_bs2.h, 32 + 16 * (k + 1))
                if k >= 1:
                    sync.wait_ge(pe_bb.h, k)
                g_bs.inc(sync.dma_start(out=rr0[0][0:1, :],
                                        in_=bsum_d.ap()[k * 512:(k + 1) * 512]), 16)

        @block.tensor
        def _(t):
            for k in range(4):
                t.wait_ge(g_bs.h, 16 * (k + 1))
                if k >= 2:
                    t.wait_ge(v_bb.h, 8 * (k - 1))
                pe_bb.inc(t.matmul(bc_ps[k % 2][:, :], ones_f[0:1, :],
                                   rr0[0][0:1, :], start=True, stop=True))

        @block.vector
        def _(v):
            for k in range(4):
                v.wait_ge(pe_bb.h, k + 1)
                for n in range(NSLOT):
                    v_bb.inc(v.tensor_tensor(
                        x_res[:, n * D + k * 512:n * D + k * 512 + 512],
                        x_res[:, n * D + k * 512:n * D + k * 512 + 512],
                        bc_ps[k % 2][:, :], ALU.add))

        @block.sync
        def _(sync):
            sync.wait_ge(v_fin.h, v_fin.n)
            for k in range(4):
                sync.wait_ge(v_bb.h, 8 * (k + 1))
                stv.inc(sync.dma_start(
                    out=bass.AP(out_t, k * 512, [[D, 128], [128 * D, 8], [1, 512]]),
                    in_=bass.AP(x_res, k * 512,
                                [[NSLOT * D, 128], [D, 8], [1, 512]])), 16)
            sync.wait_ge(stv.h, stv.n)

        ph5p.close()
        ph4s.close()

    return nc


_NC_CACHE = None


def _get_nc():
    global _NC_CACHE
    if _NC_CACHE is None:
        _NC_CACHE = build_nc()
    return _NC_CACHE


def _make_masks(j):
    m = np.zeros((NTILE, 128, 128), dtype=ml_dtypes.bfloat16)
    tri = (np.arange(128)[:, None] <= np.arange(128)[None, :]).astype(ml_dtypes.bfloat16)
    for c in range(NTILE):
        s0 = c // 8
        g0 = j + 8 * s0
        if c < g0:
            m[c, :, :] = 1.0
        elif c == g0:
            m[c, :, :] = tri
    return m


def kernel(L0_attn_x, L0_attn_W_qkv, L0_attn_b_qkv, L0_attn_W_out, L0_attn_b_out,
           L0_ffn_weight, L0_ffn_bias):
    x = np.ascontiguousarray(np.asarray(L0_attn_x, dtype=np.float32))
    wqkv = np.ascontiguousarray(np.asarray(L0_attn_W_qkv, dtype=np.float32))
    bqkv = np.ascontiguousarray(np.asarray(L0_attn_b_qkv, dtype=np.float32))
    wout = np.ascontiguousarray(np.asarray(L0_attn_W_out, dtype=np.float32))
    bout = np.ascontiguousarray(np.asarray(L0_attn_b_out, dtype=np.float32))
    wffn = np.ascontiguousarray(np.asarray(L0_ffn_weight, dtype=np.float32))
    bffn = np.ascontiguousarray(np.asarray(L0_ffn_bias, dtype=np.float32))

    ident = np.eye(128, dtype=ml_dtypes.bfloat16)

    in_maps = []
    for j in range(8):
        rows = []
        for b in range(B):
            for s in range(4):
                g = j + 8 * s
                rows.append(x[b, g * 128:(g + 1) * 128, :])
        xj = np.ascontiguousarray(np.concatenate(rows, axis=0))
        in_maps.append({
            "x": xj, "wqkv": wqkv, "bqkv": bqkv, "wout": wout, "bout": bout,
            "wffn": wffn, "bffn": bffn, "identb": ident, "masks": _make_masks(j),
        })

    nc = _get_nc()
    res = run_bass_kernel_spmd(nc, in_maps, core_ids=list(range(8)))

    out = np.empty((B, S, D), dtype=np.float32)
    for j in range(8):
        oj = res.results[j]["out"]
        for b in range(B):
            for s in range(4):
                g = j + 8 * s
                slot = b * 4 + s
                out[b, g * 128:(g + 1) * 128, :] = oj[slot * 128:(slot + 1) * 128, :]
    return out
